# version 08 (final)
# speedup vs baseline: 1.3823x; 1.0993x over previous
import contextlib
import numpy as np
import ml_dtypes
import concourse.bass as bass
import concourse.mybir as mybir
from concourse.bass_utils import run_bass_kernel_spmd

F32 = mybir.dt.float32
BF16 = mybir.dt.bfloat16
AF = mybir.ActivationFunctionType
ALU = mybir.AluOpType
AX = mybir.AxisListType
BF = ml_dtypes.bfloat16

T = 4096
D = 1024
NT = 32
H = 16
HD = 64
DBG = False
STAGES = 7
import os
B_PHASE = int(os.environ.get('B_PHASE', '9'))
B_HEADS = os.environ.get('B_HEADS', 'grouped')
Prog_NODEDUP = os.environ.get('NODEDUP') == '1'
G_PHASE = int(os.environ.get('G_PHASE', '9'))
G4 = os.environ.get('G4', 'all')
J_PHASE = int(os.environ.get('J_PHASE', '9'))
J5 = os.environ.get('J5', 'full')
STRICT_SAME = True

ENGINES = ("sync", "scalar", "vector", "tensor", "gpsimd")


class Sem:
    __slots__ = ("h", "n")

    def __init__(self, h):
        self.h = h
        self.n = 0


class Res:
    __slots__ = ("name", "w", "rs")

    def __init__(self, name):
        self.name = name
        self.w = None
        self.rs = []


class Prog:
    def __init__(self, nc):
        self.nc = nc
        self.q = {e: [] for e in ENGINES}
        self.known = {e: {} for e in ENGINES}

    def _emit(self, eng, fn):
        self.q[eng].append(fn)

    log = []

    NODEDUP = False

    def _wait(self, eng, sem, val):
        if val is None or val <= 0:
            return
        if not Prog.NODEDUP and self.known[eng].get(sem, 0) >= val:
            return
        self.known[eng][sem] = val
        h = sem.h
        Prog.log.append((eng, "wait", id(sem), val))
        self._emit(eng, lambda e, h=h, v=val: e.wait_ge(h, v))

    def step(self, eng, fns, sem, reads=(), writes=(), dma=False):
        if not isinstance(fns, (list, tuple)):
            fns = [fns]
        for r in reads:
            if r.w is not None:
                self._wait(eng, *r.w)
        for r in writes:
            if r.w is not None:
                self._wait(eng, *r.w)
            for (s, v) in r.rs:
                self._wait(eng, s, v)
        h = sem.h
        if dma:
            for fn in fns:
                self._emit(eng, lambda e, fn=fn, h=h: fn(e).then_inc(h, 16))
            sem.n += 16 * len(fns)
            v = sem.n
        else:
            for fn in fns:
                self._emit(eng, lambda e, fn=fn: fn(e))
            sem.n += 1
            v = sem.n
            self._emit(eng, lambda e, h=h: e.drain(fusable=False).then_inc(h, 1))
        if not STRICT_SAME:
            self.known[eng][sem] = v
        Prog.log.append((eng, "step", id(sem), v,
                         [r.name for r in reads], [r.name for r in writes]))
        for r in reads:
            r.rs.append((sem, v))
        for r in writes:
            r.w = (sem, v)
            r.rs = []
        return v

    def step_dmat(self, engs, fns, sem, reads=(), writes=(), dma=True):
        if not isinstance(fns, (list, tuple)):
            fns = [fns]
        k = len(engs)
        parts = [list(fns[j::k]) for j in range(k)]
        for j, eng in enumerate(engs):
            if not parts[j]:
                continue
            for r in reads:
                if r.w is not None:
                    self._wait(eng, *r.w)
            for r in writes:
                if r.w is not None:
                    self._wait(eng, *r.w)
                for (s_, v_) in r.rs:
                    self._wait(eng, s_, v_)
        h = sem.h
        for j, eng in enumerate(engs):
            for fn in parts[j]:
                self._emit(eng, lambda e, fn=fn, h=h: fn(e).then_inc(h, 16))
            sem.n += 16 * len(parts[j])
        v = sem.n
        Prog.log.append(("dmat", "step", id(sem), v,
                         [r.name for r in reads], [r.name for r in writes]))
        for r in reads:
            r.rs.append((sem, v))
        for r in writes:
            r.w = (sem, v)
            r.rs = []
        return v

    def barrier(self, sems):
        for eng in self.q:
            for s in sems:
                self._wait(eng, s, s.n)


def _dmq(stage):
    en = os.environ.get('DMATQ', '')
    return ("scalar", "sync") if stage in en.split(',') else ("scalar",)


def build_program(dbg=None, stages=STAGES):
    Prog.NODEDUP = Prog_NODEDUP
    if dbg is None:
        dbg = DBG or os.environ.get('KDBG') == '1'
    nc = bass.Bass("TRN2", target_bir_lowering=False)
    V, S, TE = nc.vector, nc.scalar, nc.tensor

    def din(name, shape, dtype=F32):
        return nc.dram_tensor(name, list(shape), dtype, kind="ExternalInput").ap()

    x_h = din("x", (T, D))
    wq_h = din("L0_Wq", (D, D))
    wk_h = din("L0_Wk", (D, D))
    wv_ev_h = din("L0_Wv_ev", (D, 1024))
    wv_od_h = din("L0_Wv_od", (D, 1024))
    wo0_h = din("L0_Wo", (D, D))
    bq_fm_h = din("L0_bq_fm", (128, 8))
    bk_fm_h = din("L0_bk_fm", (128, 8))
    bv_ev_h = din("L0_bv_ev", (1, 1024))
    bv_od_h = din("L0_bv_od", (1, 1024))
    bo0_h = din("L0_bo", (1, D))
    rtr_w_h = din("L0_rtr_w", (D, 8))
    rtr_b_h = din("L0_rtr_b", (128, 1))
    gate_w_h = din("L0_gate_w", (8, D, 512))
    up_w_h = din("L0_up_w", (8, D, 512))
    down_w_h = din("L0_down_w", (8, 512, D))
    sg_w_h = din("L0_sg_w", (D, 2048))
    su_w_h = din("L0_su_w", (D, 2048))
    sd_w_h = din("L0_sd_w", (2048, D))
    wqkv_a_h = din("L1_Wqkv_a", (D, 3 * D))
    bq_a_h = din("L1_bq_a_fm", (128, 8))
    bk_a_h = din("L1_bk_a_fm", (128, 8))
    bv_a_h = din("L1_bv_a_row", (1, D))
    wp_a_h = din("L1_Wp_a", (D, D))
    bp_a_h = din("L1_bp_a_row", (1, D))
    g1_h = din("L1_g1_fm", (128, 8))
    b1_h = din("L1_b1_fm", (128, 8))
    swa_mask_h = din("swa_mask", (128, 256), BF16)
    swa_mask0_h = din("swa_mask0", (128, 256), BF16)
    wqkv_s_h = din("L1_Wqkv_s", (D, 3 * D))
    bq_s_h = din("L1_bq_s_fm", (128, 8))
    bk_s_h = din("L1_bk_s_fm", (128, 8))
    bv_s_h = din("L1_bv_s_row", (1, D))
    wo_s_h = din("L1_Wo_s", (D, D))
    bo_s_h = din("L1_bo_s_row", (1, D))
    tril_h = din("tril_incl", (128, 128), BF16)
    wo1_h = din("L1_Wo", (D, D))
    bo1_h = din("L1_bo_row", (1, D))
    gm_h = din("L1_gm_fm", (128, 8))
    bm_h = din("L1_bm_fm", (128, 8))
    w1_h = din("L1_W1", (D, 4 * D))
    b1m_h = din("L1_b1m_row", (1, 4 * D))
    w2_h = din("L1_W2", (4 * D, D))
    b2m_h = din("L1_b2m_row", (1, D))
    wfg_h = din("L1f_w_gate", (2816, D))
    wfu_h = din("L1f_w_up", (2816, D))
    wfd_h = din("L1f_w_down", (D, 2816))
    ones_h = din("ones_row", (1, 128), BF16)
    eye_h = din("eye128", (128, 128), BF16)
    l0_strip_h = din("l0_strip", (128, 126), BF16)

    out_h = nc.dram_tensor("out", [T, D], F32, kind="ExternalOutput").ap()
    dbg_handles = {}

    def dout(name, shape, dtype=F32):
        t = nc.dram_tensor(name, list(shape), dtype, kind="ExternalOutput").ap()
        dbg_handles[name] = t
        return t

    r1_h = nc.dram_tensor("r1_scratch", [T, D], F32, kind="Internal").ap()
    mix_h = nc.dram_tensor("mix_scratch", [T, D], F32, kind="Internal").ap()
    wqg_h = nc.dram_tensor("wqg_scratch", [2816, D], BF16, kind="Internal").ap()
    wqu_h = nc.dram_tensor("wqu_scratch", [2816, D], BF16, kind="Internal").ap()
    wqd_h = nc.dram_tensor("wqd_scratch", [D, 2816], BF16, kind="Internal").ap()

    p = Prog(nc)
    with contextlib.ExitStack() as st:
        def sbuf(name, shape, dtype):
            return st.enter_context(nc.sbuf_tensor(name, list(shape), dtype))

        def mksem(name):
            return Sem(st.enter_context(nc.semaphore(name)))

        stat = sbuf("stat", (128, 128), F32)
        stat2 = sbuf("stat2", (128, 32), F32)
        R_stat2 = Res("stat2")
        gb1 = sbuf("gb1", (128, 16), F32)
        R_gb1 = Res("gb1")
        l0_scope = contextlib.ExitStack()
        xnfm = l0_scope.enter_context(nc.sbuf_tensor("xnfm", [128, 8, T], BF16, side="right"))
        M0, RSTD0, SSQ, SSX = 0, 32, 64, 96
        M1, RSTD1 = 64, 96
        rtw = sbuf("rtw", (128, NT * 8), F32)
        kfm = sbuf("kfm", (128, 8, 64), BF16)
        vbd = sbuf("vbd", (128, 8, 128), BF16)
        ones_sb = sbuf("ones_sb", (1, 128), BF16)
        eye_sb = sbuf("eye_sb", (128, 128), BF16)
        strip0 = sbuf("strip0", (128, 126), BF16)
        rtrb_sb = sbuf("rtrb", (128, 1), F32)

        s_ld = mksem("s_ld")
        s_w = mksem("s_w")
        s_sc = mksem("s_sc")
        s_v = mksem("s_v")
        s_t = mksem("s_t")
        s_dt = mksem("s_dt")
        s_st = mksem("s_st")
        ALL = [s_ld, s_w, s_sc, s_v, s_t, s_dt, s_st]

        R_xnfm = [Res(f"xnfm{i}") for i in range(NT)]
        R_psAT = Res("psAT")
        R_stat = Res("stat")
        R_rtw = Res("rtw")
        R_kfm = Res("kfm")
        R_vbd = Res("vbd")
        R_consts = Res("consts")

        psAT_cm = nc.psum_tensor("a_psT", [128, 2048], BF16)
        psAT = psAT_cm.__enter__()
        with nc.sbuf_tensor("a_xsb", [128, 4 * D], F32) as xsb, \
             nc.sbuf_tensor("a_trs", [128, 1024], BF16) as trash_s, \
             nc.sbuf_tensor("a_trv", [128, 1024], BF16) as trash_v, \
             nc.sbuf_tensor("a_xntm", [128, 2 * D], BF16) as xn_tm, \
             nc.sbuf_tensor("a_bsum", [128, 8, 64], F32) as bsum, \
             nc.sbuf_tensor("a_gcol", [128, 8, 64], F32) as gcol, \
             nc.sbuf_tensor("a_smry", [128, 8, 64], BF16) as smry, \
             nc.sbuf_tensor("a_logits", [128, NT * 8], F32) as logits, \
             nc.sbuf_tensor("a_wrt", [128, 8, 8], BF16) as wrt_sb, \
             nc.sbuf_tensor("a_rtmp", [128, NT * 8], F32) as rtmp, \
             nc.sbuf_tensor("a_ge", [128, NT * 8], F32) as ge_sb, \
             nc.sbuf_tensor("a_rs", [128, 2 * NT], F32) as rs, \
             nc.sbuf_tensor("a_wk", [128, 8, D], BF16) as wk_sb, \
             nc.sbuf_tensor("a_wv", [128, 8, 2048], BF16) as wv_sb, \
             nc.sbuf_tensor("a_brow", [1, 2048], BF16) as brow, \
             nc.sbuf_tensor("a_bk", [128, 8], F32) as bk_sb, \
             nc.psum_tensor("a_ps", [128, 2048], F32) as psA:

            R_xsb = [Res(f"xsb{j}") for j in range(4)]
            R_trs, R_trv = Res("trsA"), Res("trvA")
            R_ssq, R_ssx = Res("ssq"), Res("ssx")
            R_xntm = [Res(f"xntm{j}") for j in range(2)]
            R_bsum, R_gcol, R_smry = Res("bsum"), Res("gcol"), Res("smry")
            R_logits, R_rtmp, R_ge, R_rs = Res("lg"), Res("rtmp"), Res("ge"), Res("rs")
            R_wA = Res("wA")
            R_psA = [Res(f"psA{b}") for b in range(4)]

            p.step("gpsimd", lambda e: e.dma_start(
                out=wrt_sb[:], in_=rtr_w_h.rearrange("(c p) e -> p c e", p=128)),
                s_w, writes=[R_wA], dma=True)
            p.step("sync", lambda e: e.dma_start(out=ones_sb[:], in_=ones_h[:, :]),
                   s_w, writes=[R_consts], dma=True)
            p.step("sync", lambda e: e.dma_start(out=eye_sb[:], in_=eye_h[:, :]),
                   s_w, writes=[R_consts], dma=True)
            p.step("sync", lambda e: e.dma_start(out=strip0[:], in_=l0_strip_h[:, :]),
                   s_w, writes=[R_consts], dma=True)
            p.step("sync", lambda e: e.dma_start(out=rtrb_sb[:], in_=rtr_b_h[:, :]),
                   s_w, writes=[R_consts], dma=True)
            p.step("gpsimd", lambda e: e.dma_start(
                out=wk_sb[:], in_=wk_h.rearrange("(k p) n -> p k n", p=128)),
                s_w, writes=[R_wA], dma=True)
            p.step("gpsimd", lambda e: e.dma_start(
                out=wv_sb[:, :, 0:1024], in_=wv_ev_h.rearrange("(k p) n -> p k n", p=128)),
                s_w, writes=[R_wA], dma=True)
            p.step("gpsimd", lambda e: e.dma_start(
                out=wv_sb[:, :, 1024:2048], in_=wv_od_h.rearrange("(k p) n -> p k n", p=128)),
                s_w, writes=[R_wA], dma=True)
            p.step("gpsimd", lambda e: e.dma_start(out=brow[0:1, 0:1024], in_=bv_ev_h[:, :]),
                   s_w, writes=[R_wA], dma=True)
            p.step("gpsimd", lambda e: e.dma_start(out=brow[0:1, 1024:2048], in_=bv_od_h[:, :]),
                   s_w, writes=[R_wA], dma=True)
            p.step("sync", lambda e: e.dma_start(out=bk_sb[:], in_=bk_fm_h[:, :]),
                   s_w, writes=[R_wA], dma=True)

            for i in range(NT):
                sl = i % 4
                p.step("sync", lambda e, i=i, sl=sl: e.dma_start(
                    out=xsb[:, sl * D:(sl + 1) * D], in_=x_h[i * 128:(i + 1) * 128, :]),
                    s_ld, writes=[R_xsb[sl]], dma=True)
                p.step("scalar", lambda e, i=i, sl=sl: S.activation(
                    trash_s[:], xsb[:, sl * D:(sl + 1) * D], AF.Square,
                    accum_out=stat[:, SSQ + i:SSQ + i + 1]),
                    s_sc, reads=[R_xsb[sl]], writes=[R_trs, R_ssq])
                p.step("vector", lambda e, i=i, sl=sl: V.tensor_scalar(
                    trash_v[:], xsb[:, sl * D:(sl + 1) * D], 1.0, 0.0, ALU.mult, ALU.add,
                    accum_out=stat[:, SSX + i:SSX + i + 1]),
                    s_v, reads=[R_xsb[sl]], writes=[R_trv, R_ssx])
            p.step("vector", lambda e: V.tensor_scalar(
                stat[:, M0:M0 + NT], stat[:, SSX:SSX + NT], 1.0 / D, None, ALU.mult),
                s_v, reads=[R_ssx], writes=[R_stat])
            p.step("vector", lambda e: V.tensor_tensor(
                stat[:, RSTD0:RSTD0 + NT], stat[:, M0:M0 + NT], stat[:, M0:M0 + NT],
                ALU.mult), s_v, writes=[R_stat])
            p.step("vector", lambda e: V.scalar_tensor_tensor(
                stat[:, RSTD0:RSTD0 + NT], stat[:, SSQ:SSQ + NT], 1.0 / D,
                stat[:, RSTD0:RSTD0 + NT], ALU.mult, ALU.subtract),
                s_v, reads=[R_ssq], writes=[R_stat])
            p.step("vector", lambda e: V.tensor_scalar(
                stat[:, RSTD0:RSTD0 + NT], stat[:, RSTD0:RSTD0 + NT], 1e-5, None, ALU.add),
                s_v, writes=[R_stat])
            p.step("vector", lambda e: V.reciprocal(
                stat[:, RSTD0:RSTD0 + NT], stat[:, RSTD0:RSTD0 + NT]),
                s_v, writes=[R_stat])
            p.step("scalar", lambda e: S.activation(
                stat[:, RSTD0:RSTD0 + NT], stat[:, RSTD0:RSTD0 + NT], AF.Sqrt),
                s_sc, writes=[R_stat])

            for i in range(NT):
                sl = i % 4
                bs = i % 2
                p.step("sync", lambda e, i=i, sl=sl: e.dma_start(
                    out=xsb[:, sl * D:(sl + 1) * D], in_=x_h[i * 128:(i + 1) * 128, :]),
                    s_ld, writes=[R_xsb[sl]], dma=True)
                p.step("vector", lambda e, i=i, sl=sl, bs=bs: V.tensor_scalar(
                    xn_tm[:, bs * D:(bs + 1) * D], xsb[:, sl * D:(sl + 1) * D],
                    stat[:, M0 + i:M0 + i + 1], stat[:, RSTD0 + i:RSTD0 + i + 1],
                    ALU.subtract, ALU.mult),
                    s_v, reads=[R_xsb[sl], R_stat], writes=[R_xntm[bs]])
                p.step("tensor",
                       [lambda e, bs=bs, k=k: TE.transpose(
                           psAT[:, k * 128:(k + 1) * 128],
                           xn_tm[:, bs * D + k * 128: bs * D + (k + 1) * 128],
                           eye_sb[:])
                        for k in range(8)],
                       s_t, reads=[R_xntm[bs], R_consts], writes=[R_psAT])
                p.step("vector",
                       [lambda e, i=i, k=k: V.tensor_copy(
                           xnfm[:, k, i * 128:(i + 1) * 128],
                           psAT[:, k * 128:(k + 1) * 128])
                        for k in range(8)],
                       s_v, reads=[R_psAT], writes=[R_xnfm[i]])
                p.step("vector", lambda e, i=i: V.tensor_reduce(
                    bsum[:, :, 2 * i:2 * i + 2],
                    xnfm[:, :, i * 128:(i + 1) * 128].rearrange(
                        "p c (b w) -> p c b w", w=64), AX.X, ALU.add),
                    s_v, reads=[R_xnfm[i]], writes=[R_bsum])
                p.step("vector", lambda e, i=i: V.tensor_copy(
                    gcol[:, :, 2 * i:2 * i + 2],
                    xnfm[:, :, i * 128:(i + 1) * 128].rearrange(
                        "p c (b w) -> p c b w", w=64)[:, :, :, 0:1].squeeze(3)),
                    s_v, reads=[R_xnfm[i]], writes=[R_gcol])
                p.step("tensor",
                       [lambda e, i=i, k=k: TE.matmul(
                           psA[:, 0:8], xnfm[:, k, i * 128:(i + 1) * 128],
                           wrt_sb[:, k, :], start=(k == 0), stop=(k == 7))
                        for k in range(8)],
                       s_t, reads=[R_xnfm[i], R_wA], writes=[R_psA[0]])
                p.step("vector", lambda e, i=i: V.tensor_copy(
                    logits[:, 8 * i:8 * i + 8], psA[:, 0:8]),
                    s_v, reads=[R_psA[0]], writes=[R_logits])

            p.step("vector", lambda e: V.tensor_tensor(
                bsum[:, :, 0:63], bsum[:, :, 0:63], gcol[:, :, 0:63], ALU.subtract),
                s_v, reads=[R_gcol], writes=[R_bsum])
            p.step("vector", lambda e: V.tensor_tensor(
                bsum[:, :, 0:63], bsum[:, :, 0:63], gcol[:, :, 1:64], ALU.add),
                s_v, reads=[R_gcol], writes=[R_bsum])
            p.step("vector", lambda e: V.tensor_scalar(
                smry[:, :, 1:64], bsum[:, :, 0:63], 1.0 / 64, None, ALU.mult),
                s_v, reads=[R_bsum], writes=[R_smry])
            p.step("vector", lambda e: V.tensor_copy(
                smry[:, :, 0:1].squeeze(2), gcol[:, :, 0:1].squeeze(2)),
                s_v, reads=[R_gcol], writes=[R_smry])

            lg3 = logits[:].rearrange("p (t e) -> p t e", e=8)
            rt3 = rtw[:].rearrange("p (t e) -> p t e", e=8)
            rk3 = rtmp[:].rearrange("p (t e) -> p t e", e=8)
            g3 = ge_sb[:].rearrange("p (t e) -> p t e", e=8)
            rmax = rs[:, 0:NT]
            rsum = rs[:, NT:2 * NT]
            p.step("vector", lambda e: V.tensor_reduce(rmax, lg3, AX.X, ALU.max),
                   s_v, reads=[R_logits], writes=[R_rs])
            p.step("vector", lambda e: V.tensor_tensor(
                lg3, lg3, rmax.unsqueeze(2).broadcast_to([128, NT, 8]), ALU.subtract),
                s_v, reads=[R_rs], writes=[R_logits])
            p.step("scalar", lambda e: S.activation(logits[:], logits[:], AF.Exp),
                   s_sc, writes=[R_logits])
            p.step("vector", lambda e: V.tensor_reduce(rsum, lg3, AX.X, ALU.add),
                   s_v, reads=[R_logits], writes=[R_rs])
            p.step("vector", lambda e: V.reciprocal(rsum, rsum), s_v, writes=[R_rs])
            p.step("vector", lambda e: V.tensor_tensor(
                lg3, lg3, rsum.unsqueeze(2).broadcast_to([128, NT, 8]), ALU.mult),
                s_v, reads=[R_rs], writes=[R_logits])
            p.step("vector", lambda e: V.tensor_scalar(
                logits[:], logits[:], rtrb_sb[:, 0:1], None, ALU.add),
                s_v, reads=[R_consts], writes=[R_logits])
            p.step("vector", lambda e: V.memset(rtmp[:], 0.0), s_v, writes=[R_rtmp])
            for sh in range(1, 8):
                w = 8 - sh
                p.step("vector", lambda e, sh=sh, w=w: V.tensor_tensor(
                    g3[:, :, 0:w], lg3[:, :, 0:w], lg3[:, :, sh:8], ALU.is_ge),
                    s_v, reads=[R_logits], writes=[R_ge])
                p.step("vector", lambda e, sh=sh, w=w: V.tensor_tensor(
                    rk3[:, :, sh:8], rk3[:, :, sh:8], g3[:, :, 0:w], ALU.add),
                    s_v, reads=[R_ge], writes=[R_rtmp])
                p.step("vector", lambda e, sh=sh, w=w: V.tensor_scalar(
                    g3[:, :, 0:w], g3[:, :, 0:w], -1.0, 1.0, ALU.mult, ALU.add),
                    s_v, writes=[R_ge])
                p.step("vector", lambda e, sh=sh, w=w: V.tensor_tensor(
                    rk3[:, :, 0:w], rk3[:, :, 0:w], g3[:, :, 0:w], ALU.add),
                    s_v, reads=[R_ge], writes=[R_rtmp])
            p.step("vector", lambda e: V.tensor_scalar(
                logits[:], logits[:], rtrb_sb[:, 0:1], None, ALU.subtract),
                s_v, reads=[R_consts], writes=[R_logits])
            p.step("vector", lambda e: V.tensor_scalar(
                rtmp[:], rtmp[:], 4.0, None, ALU.is_lt), s_v, writes=[R_rtmp])
            p.step("vector", lambda e: V.tensor_tensor(
                rtw[:], logits[:], rtmp[:], ALU.mult),
                s_v, reads=[R_logits, R_rtmp], writes=[R_rtw])
            p.step("vector", lambda e: V.tensor_reduce(rsum, rt3, AX.X, ALU.add),
                   s_v, reads=[R_rtw], writes=[R_rs])
            p.step("vector", lambda e: V.tensor_scalar(
                rsum, rsum, 1e-20, None, ALU.add), s_v, writes=[R_rs])
            p.step("vector", lambda e: V.reciprocal(rsum, rsum), s_v, writes=[R_rs])
            p.step("vector", lambda e: V.tensor_tensor(
                rt3, rt3, rsum.unsqueeze(2).broadcast_to([128, NT, 8]), ALU.mult),
                s_v, reads=[R_rs], writes=[R_rtw])

            for dc in range(8):
                bank = dc % 2
                col = bank * 512
                p.step("tensor",
                       [lambda e, dc=dc, k=k, col=col, first=(k2 == 0):
                        (TE.ldweights(wk_sb[:, k, dc * 128:(dc + 1) * 128])
                         if first else
                         TE.matmul(psA[:, col:col + 64],
                                   wk_sb[:, k, dc * 128:(dc + 1) * 128],
                                   smry[:, k, :], start=(k == 0), stop=(k == 7)))
                        for k in range(8) for k2 in range(2)],
                       s_t, reads=[R_wA, R_smry], writes=[R_psA[bank]])
                p.step("vector", lambda e, dc=dc, col=col: V.tensor_scalar(
                    kfm[:, dc, :], psA[:, col:col + 64],
                    bk_sb[:, dc:dc + 1], None, ALU.add),
                    s_v, reads=[R_psA[bank]], writes=[R_kfm])

            for par in range(2):
                tp = None if par == 0 else (0, 64)
                fns = []
                for k in range(8):
                    fns.append(lambda e, k=k: TE.ldweights(smry[:, k, :]))
                    for half in range(2):
                        fns.append(lambda e, k=k, half=half, par=par, tp=tp: TE.matmul(
                            psA[par * 64:par * 64 + 64,
                                1024 + half * 512:1024 + (half + 1) * 512],
                            smry[:, k, :],
                            wv_sb[:, k, par * 1024 + half * 512:par * 1024 + (half + 1) * 512],
                            start=(k == 0), stop=False, tile_position=tp))
                fns.append(lambda e: TE.ldweights(ones_sb[0:1, 0:64]))
                for half in range(2):
                    fns.append(lambda e, par=par, half=half, tp=tp: TE.matmul(
                        psA[par * 64:par * 64 + 64,
                            1024 + half * 512:1024 + (half + 1) * 512],
                        ones_sb[0:1, 0:64],
                        brow[0:1, par * 1024 + half * 512:par * 1024 + (half + 1) * 512],
                        start=False, stop=True, tile_position=tp))
                p.step("tensor", fns, s_t,
                       reads=[R_smry, R_wA, R_consts],
                       writes=[R_psA[2], R_psA[3]])
                p.step("vector", lambda e, par=par: V.tensor_copy(
                    vbd[par * 64:par * 64 + 64, :, :],
                    psA[par * 64:par * 64 + 64, 1024:2048].rearrange(
                        "p (g d) -> p g d", d=128)),
                    s_v, reads=[R_psA[2], R_psA[3]], writes=[R_vbd])

        psAT_cm.__exit__(None, None, None)

        p.barrier(ALL)

        if stages >= 2:
            with nc.sbuf_tensor("b_wq", [128, 8, D], BF16) as wq_sb, \
                 nc.sbuf_tensor("b_wo", [128, 8, D], BF16) as wo_sb, \
                 nc.sbuf_tensor("b_bo", [1, D], BF16) as bo_row, \
                 nc.sbuf_tensor("b_bq", [128, 8], F32) as bq_sb, \
                 nc.sbuf_tensor("b_qfm", [128, 8, 512], BF16) as qfm, \
                 nc.sbuf_tensor("b_att", [128, 2, 16, 64], BF16) as att, \
                 nc.sbuf_tensor("b_den", [128, 2, 16], F32) as den, \
                 nc.sbuf_tensor("b_at", [128, 2, 8, 128], BF16) as at_sb, \
                 nc.sbuf_tensor("b_otm", [128, 2, D], BF16) as otm, \
                 nc.sbuf_tensor("b_ofm", [128, 2, 8, 128], BF16) as ofm, \
                 nc.sbuf_tensor("b_x", [128, 2 * D], F32) as xb, \
                 nc.sbuf_tensor("b_r1", [128, 2 * D], F32) as r1sb, \
                 nc.psum_tensor("b_ps", [128, 3072], F32) as psB, \
                 nc.psum_tensor("b_psT", [128, 2048], BF16) as psBT:
                R_wB = Res("wB")
                R_q = [Res(f"q{b}") for b in range(2)]
                R_qfm = Res("qfm")
                R_ps_s = Res("ps_scores")
                R_ps_o = Res("ps_o")
                R_psBT = Res("psBT")
                R_att = [Res(f"att{j}") for j in range(2)]
                R_den = [Res(f"den{j}") for j in range(2)]
                R_at = [Res(f"at{j}") for j in range(2)]
                R_otm = [Res(f"otm{j}") for j in range(2)]
                R_ofm = [Res(f"ofm{j}") for j in range(2)]
                R_xb = [Res(f"xb{j}") for j in range(2)]
                R_r1 = [Res(f"r1{j}") for j in range(2)]
                R_r1h = [Res(f"r1h{i}") for i in range(NT)]

                p.step("gpsimd", lambda e: e.dma_start(
                    out=wq_sb[:], in_=wq_h.rearrange("(k p) n -> p k n", p=128)),
                    s_w, writes=[R_wB], dma=True)
                p.step("gpsimd", lambda e: e.dma_start(
                    out=wo_sb[:], in_=wo0_h.rearrange("(k p) n -> p k n", p=128)),
                    s_w, writes=[R_wB], dma=True)
                p.step("gpsimd", lambda e: e.dma_start(out=bo_row[:], in_=bo0_h[:, :]),
                       s_w, writes=[R_wB], dma=True)
                p.step("sync", lambda e: e.dma_start(out=bq_sb[:], in_=bq_fm_h[:, :]),
                       s_w, writes=[R_wB], dma=True)

                for i in range(NT):
                    g, gi, sl = i // 4, i % 4, i % 2
                    p._wait("scalar", s_t, s_t.n)
                    if gi == 0 and B_PHASE >= 1:
                        for dc in range(8):
                            bank = dc % 2
                            col = bank * 512
                            p.step("tensor",
                                   [lambda e, g=g, dc=dc, k=k, col=col, first=(k2 == 0):
                                    (TE.ldweights(wq_sb[:, k, dc * 128:(dc + 1) * 128])
                                     if first else
                                     TE.matmul(psB[:, col:col + 512],
                                               wq_sb[:, k, dc * 128:(dc + 1) * 128],
                                               xnfm[:, k, g * 512:(g + 1) * 512],
                                               start=(k == 0), stop=(k == 7)))
                                    for k in range(8) for k2 in range(2)],
                                   s_t,
                                   reads=[R_wB] + [R_xnfm[4 * g + j] for j in range(4)],
                                   writes=[R_q[bank]])
                            p.step("vector", lambda e, dc=dc, col=col: V.tensor_scalar(
                                qfm[:, dc, :], psB[:, col:col + 512],
                                bq_sb[:, dc:dc + 1], None, ALU.add),
                                s_v, reads=[R_q[bank], R_wB], writes=[R_qfm])

                    if B_PHASE >= 2:
                        p.step("tensor",
                               [lambda e, h=h, gi=gi, first=(k2 == 0),
                                bp=(h % 2) * 64, tp=((h % 2) * 64, 0):
                                (TE.ldweights(qfm[bp:bp + 64, h // 2,
                                                  gi * 128:(gi + 1) * 128],
                                              tile_position=tp)
                                 if first else
                                 TE.matmul(psB[:, 1024 + h * 64:1024 + (h + 1) * 64],
                                           qfm[bp:bp + 64, h // 2, gi * 128:(gi + 1) * 128],
                                           kfm[bp:bp + 64, h // 2, :],
                                           start=True, stop=True, tile_position=tp))
                                for h in ({'all': range(16), 'even': range(0, 16, 2),
                                           'odd': range(1, 16, 2),
                                           'grouped': list(range(0, 16, 2)) + list(range(1, 16, 2))}[B_HEADS])
                                for k2 in range(2)],
                               s_t, reads=[R_qfm, R_kfm], writes=[R_ps_s])
                    if B_PHASE >= 3:
                        p.step("scalar", lambda e, sl=sl: S.activation(
                            att[:, sl], psB[:, 1024:2048].rearrange("p (h c) -> p h c", c=64),
                            AF.Exp, scale=0.125),
                            s_sc, reads=[R_ps_s], writes=[R_att[sl]])
                    if B_PHASE >= 4:
                        off = 62 - 2 * i
                        p.step("vector", lambda e, sl=sl, off=off: V.tensor_tensor(
                            att[:, sl], att[:, sl],
                            strip0[:, off:off + 64].unsqueeze(1).broadcast_to([128, 16, 64]),
                            ALU.mult),
                            s_v, reads=[R_consts], writes=[R_att[sl]])
                        p.step("vector", lambda e, sl=sl: V.tensor_reduce(
                            den[:, sl], att[:, sl], AX.X, ALU.add),
                            s_v, reads=[R_att[sl]], writes=[R_den[sl]])
                        p.step("vector", lambda e, sl=sl: V.reciprocal(
                            den[:, sl], den[:, sl]), s_v, writes=[R_den[sl]])
                        p.step("vector", lambda e, sl=sl: V.tensor_tensor(
                            att[:, sl], att[:, sl],
                            den[:, sl].unsqueeze(2).broadcast_to([128, 16, 64]), ALU.mult),
                            s_v, reads=[R_den[sl]], writes=[R_att[sl]])
                    if B_PHASE >= 5:
                        p.step("tensor",
                               [lambda e, sl=sl, gq=gq: TE.transpose(
                                   psBT[:, gq * 128:(gq + 1) * 128],
                                   att[:, sl].rearrange("p h c -> p (h c)")[
                                       :, gq * 128:(gq + 1) * 128],
                                   eye_sb[:])
                                for gq in range(8)],
                               s_t, reads=[R_att[sl], R_consts], writes=[R_psBT])
                        p.step("vector", lambda e, sl=sl: V.tensor_copy(
                            at_sb[:, sl].rearrange("p a b -> p (a b)"),
                            psBT[:, 0:1024]),
                            s_v, reads=[R_psBT], writes=[R_at[sl]])
                    if B_PHASE >= 6:
                        p.step("tensor",
                               [lambda e, sl=sl, gq=gq, first=(k2 == 0):
                                (TE.ldweights(at_sb[:, sl, gq, :]) if first else
                                 TE.matmul(psB[:, 2048 + gq * 128:2048 + (gq + 1) * 128],
                                           at_sb[:, sl, gq, :], vbd[:, gq, :],
                                           start=True, stop=True))
                                for gq in range(8) for k2 in range(2)],
                               s_t, reads=[R_at[sl], R_vbd], writes=[R_ps_o])
                        p.step("vector", lambda e, sl=sl: V.tensor_copy(
                            otm[:, sl], psB[:, 2048:3072]),
                            s_v, reads=[R_ps_o], writes=[R_otm[sl]])
                    if B_PHASE >= 7:
                        p.step("tensor",
                               [lambda e, sl=sl, k=k: TE.transpose(
                                   psBT[:, k * 128:(k + 1) * 128],
                                   otm[:, sl, k * 128:(k + 1) * 128], eye_sb[:])
                                for k in range(8)],
                               s_t, reads=[R_otm[sl], R_consts], writes=[R_psBT])
                        p.step("vector", lambda e, sl=sl: V.tensor_copy(
                            ofm[:, sl].rearrange("p a b -> p (a b)"),
                            psBT[:, 0:1024]),
                            s_v, reads=[R_psBT], writes=[R_ofm[sl]])
                    if B_PHASE >= 8:
                        p.step("sync", lambda e, i=i, sl=sl: e.dma_start(
                            out=xb[:, sl * D:(sl + 1) * D],
                            in_=x_h[i * 128:(i + 1) * 128, :]),
                            s_ld, writes=[R_xb[sl]], dma=True)
                        fns = []
                        for k in range(8):
                            fns.append(lambda e, sl=sl, k=k: TE.ldweights(ofm[:, sl, k, :]))
                            for half in range(2):
                                fns.append(lambda e, sl=sl, k=k, half=half: TE.matmul(
                                    psB[:, 2048 + half * 512:2048 + (half + 1) * 512],
                                    ofm[:, sl, k, :], wo_sb[:, k, half * 512:(half + 1) * 512],
                                    start=(k == 0), stop=False))
                        fns.append(lambda e: TE.ldweights(ones_sb[0:1, :]))
                        for half in range(2):
                            fns.append(lambda e, half=half: TE.matmul(
                                psB[:, 2048 + half * 512:2048 + (half + 1) * 512],
                                ones_sb[0:1, :], bo_row[0:1, half * 512:(half + 1) * 512],
                                start=False, stop=True))
                        p.step("tensor", fns, s_t,
                               reads=[R_ofm[sl], R_wB, R_consts], writes=[R_ps_o])
                    if dbg and i == 0:
                        d_wq = dout("d_wq", (128, 8 * 1024))
                        p.step("gpsimd", lambda e: e.dma_start(
                            out=d_wq[:, :], in_=wq_sb[:].rearrange("p k n -> p (k n)")),
                            s_st, reads=[R_wB], writes=[Res("dwq")], dma=True)
                        d_q = dout("d_q", (128, 8 * 512))
                        p.step("gpsimd", lambda e: e.dma_start(
                            out=d_q[:, :], in_=qfm[:].rearrange("p c n -> p (c n)")),
                            s_st, reads=[R_qfm], writes=[Res("dq")], dma=True)
                        d_att = dout("d_att", (128, 1024)) if B_PHASE >= 3 else None
                        if d_att is not None:
                            p.step("gpsimd", lambda e: e.dma_start(
                                out=d_att[:, :], in_=att[:, 0].rearrange("p h c -> p (h c)")),
                                s_st, reads=[R_att[0]], writes=[Res("datt")], dma=True)
                        d_otm = dout("d_otm", (128, 1024)) if B_PHASE >= 6 else None
                        if d_otm is not None:
                            p.step("gpsimd", lambda e: e.dma_start(
                                out=d_otm[:, :], in_=otm[:, 0]),
                                s_st, reads=[R_otm[0]], writes=[Res("dotm")], dma=True)
                    if B_PHASE >= 9:
                        p.step("vector", lambda e, sl=sl: V.tensor_tensor(
                            r1sb[:, sl * D:(sl + 1) * D], psB[:, 2048:3072],
                            xb[:, sl * D:(sl + 1) * D], ALU.add),
                            s_v, reads=[R_ps_o, R_xb[sl]], writes=[R_r1[sl]])
                        p.step("sync", lambda e, i=i, sl=sl: e.dma_start(
                            out=r1_h[i * 128:(i + 1) * 128, :],
                            in_=r1sb[:, sl * D:(sl + 1) * D]),
                            s_st, reads=[R_r1[sl]], writes=[R_r1h[i]], dma=True)
            p.barrier(ALL)

        if stages >= 3:
            with nc.sbuf_tensor("d_gate", [128, 2, 8, 1024], BF16) as gsb, \
                 nc.sbuf_tensor("d_up", [128, 2, 8, 1024], BF16) as usb, \
                 nc.sbuf_tensor("d_down", [128, 2, 8, 1024], BF16) as dsb, \
                 nc.sbuf_tensor("d_actg", [128, 2, 1024], BF16) as actg, \
                 nc.sbuf_tensor("d_acttm", [128, 2, 1024], BF16) as acttm, \
                 nc.sbuf_tensor("d_actfm", [128, 2, 8, 128], BF16) as actfm, \
                 nc.sbuf_tensor("d_f0", [128, 2, 1024], F32) as f0sb, \
                 nc.psum_tensor("d_ps", [128, 3072], F32) as psD, \
                 nc.psum_tensor("d_psT", [128, 2048], BF16) as psDT:
                R_wD = [Res(f"wD{j}") for j in range(2)]
                R_ps_g = Res("ps_g")
                R_ps_u = Res("ps_u")
                R_ps_d = Res("ps_d")
                R_actg = [Res(f"actg{j}") for j in range(2)]
                R_acttm = [Res(f"acttm{j}") for j in range(2)]
                R_actfm = [Res(f"actfm{j}") for j in range(2)]
                R_psDT = Res("psDT")
                R_f0 = [Res(f"f0{j}") for j in range(2)]

                subs = []
                for ei in range(4):
                    subs.append(("exp", (2 * ei, 2 * ei + 1)))
                for hf in range(2):
                    subs.append(("sh", hf))

                for si, (kind, pay) in enumerate(subs):
                    ws = si % 2
                    if kind == "exp":
                        e0, e1 = pay
                        for j, ee in enumerate((e0, e1)):
                            p.step("gpsimd", lambda e, ws=ws, j=j, ee=ee: e.dma_start(
                                out=gsb[:, ws, :, j * 512:(j + 1) * 512],
                                in_=gate_w_h[ee].rearrange("(k p) n -> p k n", p=128)),
                                s_w, writes=[R_wD[ws]], dma=True)
                            p.step("gpsimd", lambda e, ws=ws, j=j, ee=ee: e.dma_start(
                                out=usb[:, ws, :, j * 512:(j + 1) * 512],
                                in_=up_w_h[ee].rearrange("(k p) n -> p k n", p=128)),
                                s_w, writes=[R_wD[ws]], dma=True)
                            p.step("gpsimd", lambda e, ws=ws, j=j, ee=ee: e.dma_start(
                                out=dsb[:, ws, 4 * j:4 * j + 4, :],
                                in_=down_w_h[ee].rearrange("(k p) n -> p k n", p=128)),
                                s_w, writes=[R_wD[ws]], dma=True)
                    else:
                        hf = pay
                        p.step("gpsimd", lambda e, ws=ws, hf=hf: e.dma_start(
                            out=gsb[:, ws, :, :],
                            in_=sg_w_h.rearrange("(k p) n -> p k n", p=128)[
                                :, :, hf * 1024:(hf + 1) * 1024]),
                            s_w, writes=[R_wD[ws]], dma=True)
                        p.step("gpsimd", lambda e, ws=ws, hf=hf: e.dma_start(
                            out=usb[:, ws, :, :],
                            in_=su_w_h.rearrange("(k p) n -> p k n", p=128)[
                                :, :, hf * 1024:(hf + 1) * 1024]),
                            s_w, writes=[R_wD[ws]], dma=True)
                        p.step("gpsimd", lambda e, ws=ws, hf=hf: e.dma_start(
                            out=dsb[:, ws, :, :],
                            in_=sd_w_h[hf * 1024:(hf + 1) * 1024, :].rearrange(
                                "(k p) n -> p k n", p=128)),
                            s_w, writes=[R_wD[ws]], dma=True)

                    for i in range(NT):
                        sl = i % 2
                        p._wait("scalar", s_t, s_t.n)
                        fns = []
                        for k in range(8):
                            fns.append(lambda e, i=i, k=k: TE.ldweights(
                                xnfm[:, k, i * 128:(i + 1) * 128]))
                            for half in range(2):
                                fns.append(lambda e, i=i, k=k, half=half, ws=ws: TE.matmul(
                                    psD[:, half * 512:(half + 1) * 512],
                                    xnfm[:, k, i * 128:(i + 1) * 128],
                                    gsb[:, ws, k, half * 512:(half + 1) * 512],
                                    start=(k == 0), stop=(k == 7)))
                            for half in range(2):
                                fns.append(lambda e, i=i, k=k, half=half, ws=ws: TE.matmul(
                                    psD[:, 1024 + half * 512:1024 + (half + 1) * 512],
                                    xnfm[:, k, i * 128:(i + 1) * 128],
                                    usb[:, ws, k, half * 512:(half + 1) * 512],
                                    start=(k == 0), stop=(k == 7)))
                        p.step("tensor", fns, s_t,
                               reads=[R_xnfm[i], R_wD[ws]], writes=[R_ps_g, R_ps_u])
                        p.step("scalar", lambda e, sl=sl: S.activation(
                            actg[:, sl], psD[:, 0:1024], AF.Silu),
                            s_sc, reads=[R_ps_g], writes=[R_actg[sl]])
                        if kind == "exp":
                            for j, ee in enumerate(pay):
                                p.step("vector", lambda e, sl=sl, j=j, ee=ee, i=i:
                                       V.scalar_tensor_tensor(
                                           acttm[:, sl, j * 512:(j + 1) * 512],
                                           psD[:, 1024 + j * 512:1024 + (j + 1) * 512],
                                           rtw[:, 8 * i + ee:8 * i + ee + 1],
                                           actg[:, sl, j * 512:(j + 1) * 512],
                                           ALU.mult, ALU.mult),
                                       s_v, reads=[R_ps_u, R_actg[sl], R_rtw],
                                       writes=[R_acttm[sl]])
                        else:
                            p.step("vector", lambda e, sl=sl: V.scalar_tensor_tensor(
                                acttm[:, sl, :], psD[:, 1024:2048], 1.0,
                                actg[:, sl, :], ALU.mult, ALU.mult),
                                s_v, reads=[R_ps_u, R_actg[sl]], writes=[R_acttm[sl]])
                        p.step("tensor",
                               [lambda e, sl=sl, k=k: TE.transpose(
                                   psDT[:, k * 128:(k + 1) * 128],
                                   acttm[:, sl, k * 128:(k + 1) * 128],
                                   eye_sb[:])
                                for k in range(8)],
                               s_t, reads=[R_acttm[sl], R_consts], writes=[R_psDT])
                        p.step("vector", lambda e, sl=sl: V.tensor_copy(
                            actfm[:, sl, :, :],
                            psDT[:, 0:1024].rearrange("p (a b) -> p a b", b=128)),
                            s_v, reads=[R_psDT], writes=[R_actfm[sl]])
                        fns = []
                        for ic in range(8):
                            fns.append(lambda e, sl=sl, ic=ic: TE.ldweights(
                                actfm[:, sl, ic, :]))
                            for half in range(2):
                                fns.append(lambda e, sl=sl, ic=ic, half=half, ws=ws:
                                           TE.matmul(
                                               psD[:, 2048 + half * 512:2048 + (half + 1) * 512],
                                               actfm[:, sl, ic, :],
                                               dsb[:, ws, ic, half * 512:(half + 1) * 512],
                                               start=(ic == 0), stop=(ic == 7)))
                        p.step("tensor", fns, s_t,
                               reads=[R_actfm[sl], R_wD[ws]], writes=[R_ps_d])
                        p.step("vector", lambda e, sl=sl: V.tensor_copy(
                            f0sb[:, sl, :], psD[:, 2048:3072]),
                            s_v, reads=[R_ps_d], writes=[R_f0[sl]])
                        p.step("gpsimd", lambda e, i=i, sl=sl: e.dma_start(
                            out=r1_h[i * 128:(i + 1) * 128, :], in_=f0sb[:, sl, :],
                            accum_op=ALU.add),
                            s_st, reads=[R_f0[sl]], writes=[R_r1h[i]], dma=True)
            p.barrier(ALL)
        if stages >= 4:
            l0_scope.close()

        if stages >= 4:
            with contextlib.ExitStack() as gst:
                xninG = gst.enter_context(nc.sbuf_tensor("g_xnin", [128, 8, 4, 128], BF16))
                wqa = gst.enter_context(nc.sbuf_tensor("g_wqa", [128, 8, D], BF16))
                wka = gst.enter_context(nc.sbuf_tensor("g_wka", [128, 8, D], BF16))
                wva = gst.enter_context(nc.sbuf_tensor("g_wva", [128, 8, D], BF16))
                wpa = gst.enter_context(nc.sbuf_tensor("g_wpa", [128, 8, D], BF16))
                browG = gst.enter_context(nc.sbuf_tensor("g_brows", [1, 2 * D], BF16))
                bqkG = gst.enter_context(nc.sbuf_tensor("g_bqk", [128, 16], F32))
                maskG = gst.enter_context(nc.sbuf_tensor("g_masks", [128, 2, 256], BF16))
                r1G = gst.enter_context(nc.sbuf_tensor("g_r1", [128, 2, D], F32))
                utmG = gst.enter_context(nc.sbuf_tensor("g_utm", [128, 1, D], BF16))
                ufmG = gst.enter_context(nc.sbuf_tensor("g_ufm", [128, 1, 8, 128], BF16))
                sumsG = gst.enter_context(nc.sbuf_tensor("g_sums", [128, 2, 8], F32))
                qaG = gst.enter_context(nc.sbuf_tensor("g_qa", [128, 8, 512], BF16))
                kaG = gst.enter_context(nc.sbuf_tensor("g_ka", [128, 2, 16, 512], BF16))
                vaG = gst.enter_context(nc.sbuf_tensor("g_va", [128, 5, D], BF16))
                aeG = gst.enter_context(nc.sbuf_tensor("g_ae", [128, 1, 16, 256], BF16))
                denG = gst.enter_context(nc.sbuf_tensor("g_den", [128, 2, 16], F32))
                atG = gst.enter_context(nc.sbuf_tensor("g_at", [128, 1, 16, 2, 128], BF16))
                otmG = gst.enter_context(nc.sbuf_tensor("g_otm", [128, 1, D], BF16))
                ofmG = gst.enter_context(nc.sbuf_tensor("g_ofm", [128, 1, 8, 128], BF16))
                mixG = gst.enter_context(nc.sbuf_tensor("g_mix", [128, 1, D], F32))
                psG = gst.enter_context(nc.psum_tensor("g_ps", [128, 3072], F32))
                psGT = gst.enter_context(nc.psum_tensor("g_psT", [128, 2048], BF16))
                R_wG = Res("wG")
                R_r1G = [Res(f"r1G{j}") for j in range(2)]
                R_xnin = [Res(f"xnin{j}") for j in range(4)]
                R_utmG = [Res("utmG0")]
                R_ufmG = [Res("ufmG0")]
                R_sumsG = [Res(f"sumsG{j}") for j in range(2)]
                R_qaG = Res("qaG")
                R_kaG = [Res(f"kaG{j}") for j in range(2)]
                R_vaG = [Res(f"vaG{j}") for j in range(5)]
                R_aeG = [Res("aeG0")]
                R_denG = [Res(f"denG{j}") for j in range(2)]
                R_atG = [Res("atG0")]
                R_otmG = [Res("otmG0")]
                R_ofmG = [Res("ofmG0")]
                R_mixG = [Res("mixG0")]
                R_mixh = [Res(f"mixh{i}") for i in range(NT)]
                R_ps_sc = Res("ps_scG")
                R_ps_oG = Res("ps_oG")
                R_psGT = Res("psGT")
                R_ps_qkv = Res("ps_qkvG")

                p.step("gpsimd", lambda e: e.dma_start(
                    out=wqa[:], in_=wqkv_a_h.rearrange("(k p) n -> p k n", p=128)[:, :, 0:D]),
                    s_w, writes=[R_wG], dma=True)
                p.step("gpsimd", lambda e: e.dma_start(
                    out=wka[:], in_=wqkv_a_h.rearrange("(k p) n -> p k n", p=128)[:, :, D:2 * D]),
                    s_w, writes=[R_wG], dma=True)
                p.step("gpsimd", lambda e: e.dma_start(
                    out=wva[:], in_=wqkv_a_h.rearrange("(k p) n -> p k n", p=128)[:, :, 2 * D:3 * D]),
                    s_w, writes=[R_wG], dma=True)
                p.step("gpsimd", lambda e: e.dma_start(
                    out=wpa[:], in_=wp_a_h.rearrange("(k p) n -> p k n", p=128)),
                    s_w, writes=[R_wG], dma=True)
                p.step("gpsimd", lambda e: e.dma_start(out=browG[0:1, 0:D], in_=bv_a_h[:, :]),
                       s_w, writes=[R_wG], dma=True)
                p.step("gpsimd", lambda e: e.dma_start(out=browG[0:1, D:2 * D], in_=bp_a_h[:, :]),
                       s_w, writes=[R_wG], dma=True)
                p.step("sync", lambda e: e.dma_start(out=bqkG[:, 0:8], in_=bq_a_h[:, :]),
                       s_w, writes=[R_wG], dma=True)
                p.step("sync", lambda e: e.dma_start(out=bqkG[:, 8:16], in_=bk_a_h[:, :]),
                       s_w, writes=[R_wG], dma=True)
                p.step("sync", lambda e: e.dma_start(out=gb1[:, 0:8], in_=g1_h[:, :]),
                       s_w, writes=[R_gb1], dma=True)
                p.step("sync", lambda e: e.dma_start(out=gb1[:, 8:16], in_=b1_h[:, :]),
                       s_w, writes=[R_gb1], dma=True)
                p.step("sync", lambda e: e.dma_start(out=maskG[:, 0, :], in_=swa_mask0_h[:, :]),
                       s_w, writes=[R_wG], dma=True)
                p.step("sync", lambda e: e.dma_start(out=maskG[:, 1, :], in_=swa_mask_h[:, :]),
                       s_w, writes=[R_wG], dma=True)

                p.step("vector", lambda e: V.memset(kaG[:], 0.0), s_v,
                       writes=[R_kaG[0], R_kaG[1]])

                def ln_tile_stats(eng_buf, src_ap, sums_ap, R_src, R_sums, mcol, rcol):
                    tr = aeG[:, 0].rearrange("p h c -> p (h c)")
                    p.step("scalar", lambda e, tr=tr: S.activation(
                        tr[:, 0:1024], src_ap, AF.Square, accum_out=sums_ap[:, 0:1]),
                        s_sc, reads=[R_src], writes=[R_aeG[0], R_sums])
                    p.step("vector", lambda e, tr=tr: V.tensor_scalar(
                        tr[:, 1024:2048], src_ap, 1.0, 0.0, ALU.mult, ALU.add,
                        accum_out=sums_ap[:, 1:2]),
                        s_v, reads=[R_src], writes=[R_aeG[0], R_sums])
                    p.step("vector", lambda e: V.tensor_scalar(
                        sums_ap[:, mcol:mcol + 1], sums_ap[:, 1:2], 1.0 / D, None,
                        ALU.mult), s_v, writes=[R_sums])
                    p.step("vector", lambda e: V.tensor_scalar(
                        sums_ap[:, 2:3], sums_ap[:, mcol:mcol + 1],
                        sums_ap[:, mcol:mcol + 1], None, ALU.mult),
                        s_v, writes=[R_sums])
                    p.step("vector", lambda e: V.scalar_tensor_tensor(
                        sums_ap[:, 2:3], sums_ap[:, 0:1], 1.0 / D, sums_ap[:, 2:3],
                        ALU.mult, ALU.subtract), s_v, writes=[R_sums])
                    p.step("vector", lambda e: V.tensor_scalar(
                        sums_ap[:, 2:3], sums_ap[:, 2:3], 1e-5, None, ALU.add),
                        s_v, writes=[R_sums])
                    p.step("vector", lambda e: V.reciprocal(
                        sums_ap[:, 2:3], sums_ap[:, 2:3]), s_v, writes=[R_sums])
                    p.step("scalar", lambda e: S.activation(
                        sums_ap[:, rcol:rcol + 1], sums_ap[:, 2:3], AF.Sqrt),
                        s_sc, writes=[R_sums])

                for i in range(NT):
                    g, gi, sl = i // 4, i % 4, i % 2
                    gsl = g % 2
                    p._wait("scalar", s_t, s_t.n)
                    p.step("sync", lambda e, i=i, sl=sl: e.dma_start(
                        out=r1G[:, sl, :], in_=r1_h[i * 128:(i + 1) * 128, :]),
                        s_ld, reads=[R_r1h[i]], writes=[R_r1G[sl]], dma=True)
                    ln_tile_stats("", r1G[:, sl, :], sumsG[:, sl], R_r1G[sl],
                                  R_sumsG[sl], 4, 5)
                    p.step("vector", lambda e, i=i, sl=sl: V.tensor_copy(
                        stat[:, M1 + i:M1 + i + 1], sumsG[:, sl, 4:5]),
                        s_v, reads=[R_sumsG[sl]], writes=[R_stat])
                    p.step("vector", lambda e, i=i, sl=sl: V.tensor_copy(
                        stat[:, RSTD1 + i:RSTD1 + i + 1], sumsG[:, sl, 5:6]),
                        s_v, reads=[R_sumsG[sl]], writes=[R_stat])
                    p.step("vector", lambda e, sl=sl: V.tensor_scalar(
                        sumsG[:, sl, 6:7], sumsG[:, sl, 5:6], sumsG[:, sl, 5:6],
                        None, ALU.mult),
                        s_v, reads=[R_sumsG[sl]], writes=[R_sumsG[sl]])
                    p.step("vector", lambda e, sl=sl: V.tensor_scalar(
                        sumsG[:, sl, 6:7], sumsG[:, sl, 6:7], -1e-5, 1.0 + 1e-5,
                        ALU.mult, ALU.add), s_v, writes=[R_sumsG[sl]])
                    p.step("vector", lambda e, sl=sl: V.reciprocal(
                        sumsG[:, sl, 6:7], sumsG[:, sl, 6:7]), s_v, writes=[R_sumsG[sl]])
                    p.step("scalar", lambda e, sl=sl: S.activation(
                        sumsG[:, sl, 6:7], sumsG[:, sl, 6:7], AF.Sqrt),
                        s_sc, writes=[R_sumsG[sl]])
                    p.step("vector", lambda e, sl=sl: V.tensor_scalar(
                        sumsG[:, sl, 7:8], sumsG[:, sl, 5:6], sumsG[:, sl, 6:7],
                        None, ALU.mult), s_v, writes=[R_sumsG[sl]])
                    p.step("vector", lambda e, sl=sl: V.tensor_scalar(
                        utmG[:, 0, :], r1G[:, sl, :], sumsG[:, sl, 4:5],
                        sumsG[:, sl, 7:8], ALU.subtract, ALU.mult),
                        s_v, reads=[R_r1G[sl], R_sumsG[sl]], writes=[R_utmG[0]])
                    p.step("tensor",
                           [lambda e, k=k: TE.transpose(
                               psGT[:, k * 128:(k + 1) * 128],
                               utmG[:, 0, k * 128:(k + 1) * 128], eye_sb[:])
                            for k in range(8)],
                           s_t, reads=[R_utmG[0], R_consts], writes=[R_psGT])
                    p.step("vector", lambda e: V.tensor_copy(
                        ufmG[:, 0, :, :],
                        psGT[:, 0:1024].rearrange("p (a b) -> p a b", b=128)),
                        s_v, reads=[R_psGT], writes=[R_ufmG[0]])
                    p.step("vector", lambda e, i=i, sl=sl: V.tensor_copy(
                        stat2[:, i:i + 1], sumsG[:, sl, 7:8]),
                        s_v, reads=[R_sumsG[sl]], writes=[R_stat2])
                    p.step("vector",
                           [lambda e, i=i, k=k: V.tensor_scalar(
                               xninG[:, k, i % 4, :], ufmG[:, 0, k, :],
                               gb1[:, k:k + 1], gb1[:, 8 + k:8 + k + 1],
                               ALU.mult, ALU.add)
                            for k in range(8)],
                           s_v, reads=[R_ufmG[0], R_gb1], writes=[R_xnin[i % 4]])

                    if G_PHASE < 2:
                        continue
                    fns = []
                    for k in range(8):
                        fns.append(lambda e, i=i, k=k: TE.ldweights(
                            xninG[:, k, i % 4, :]))
                        for half in range(2):
                            fns.append(lambda e, i=i, k=k, half=half: TE.matmul(
                                psG[:, 2048 + half * 512:2048 + (half + 1) * 512],
                                xninG[:, k, i % 4, :],
                                wva[:, k, half * 512:(half + 1) * 512],
                                start=(k == 0), stop=False))
                    fns.append(lambda e: TE.ldweights(ones_sb[0:1, :]))
                    for half in range(2):
                        fns.append(lambda e, half=half: TE.matmul(
                            psG[:, 2048 + half * 512:2048 + (half + 1) * 512],
                            ones_sb[0:1, :], browG[0:1, half * 512:(half + 1) * 512],
                            start=False, stop=True))
                    p.step("tensor", fns, s_t,
                           reads=[R_xnin[i % 4], R_wG, R_consts], writes=[R_ps_qkv])
                    p.step("vector", lambda e, i=i: V.tensor_copy(
                        vaG[:, i % 5, :], psG[:, 2048:3072]),
                        s_v, reads=[R_ps_qkv], writes=[R_vaG[i % 5]])

                    if gi == 3 and G_PHASE >= 3:
                        for dc in range(8):
                            for which, wmat, dst, Rdst, bcol in (
                                    ("q", wqa, qaG, R_qaG, dc),
                                    ("k", wka, kaG, R_kaG[gsl], 8 + dc)):
                                fns = []
                                for k in range(8):
                                    fns.append(lambda e, k=k, dc=dc, wmat=wmat:
                                               TE.ldweights(wmat[:, k, dc * 128:(dc + 1) * 128]))
                                    fns.append(lambda e, g=g, k=k, dc=dc, wmat=wmat:
                                               TE.matmul(
                                                   psG[:, 2048:2560],
                                                   wmat[:, k, dc * 128:(dc + 1) * 128],
                                                   xninG[:, k, :, :].rearrange(
                                                       "p a b -> p (a b)"),
                                                   start=(k == 0), stop=(k == 7)))
                                p.step("tensor", fns, s_t,
                                       reads=[R_wG] + [R_xnin[j] for j in range(4)],
                                       writes=[R_ps_qkv])
                                if which == "q":
                                    p.step("vector", lambda e, dc=dc, bcol=bcol:
                                           V.tensor_scalar(
                                               qaG[:, dc, :], psG[:, 2048:2560],
                                               bqkG[:, bcol:bcol + 1], None, ALU.add),
                                           s_v, reads=[R_ps_qkv, R_wG], writes=[R_qaG])
                                else:
                                    p.step("vector", lambda e, gsl=gsl, dc=dc, bcol=bcol:
                                           V.tensor_scalar(
                                               kaG[0:64, gsl, 2 * dc, :],
                                               psG[0:64, 2048:2560],
                                               bqkG[0:64, bcol:bcol + 1], None, ALU.add),
                                           s_v, reads=[R_ps_qkv, R_wG], writes=[R_kaG[gsl]])
                                    p.step("vector", lambda e, gsl=gsl, dc=dc, bcol=bcol:
                                           V.tensor_scalar(
                                               kaG[64:128, gsl, 2 * dc + 1, :],
                                               psG[64:128, 2048:2560],
                                               bqkG[64:128, bcol:bcol + 1], None, ALU.add),
                                           s_v, reads=[R_ps_qkv, R_wG], writes=[R_kaG[gsl]])

                        for ii in (range(4 * g, 4 * g + 4) if G_PHASE >= 4 else []):
                            isl = ii % 2
                            igi = ii % 4
                            for hg in range(4):
                                fns = []
                                if ii == 0:
                                    fns.append(lambda e, hg=hg: V.memset(
                                        psG[:, 0:1024].rearrange(
                                            "p (h b k) -> p h b k", h=4, b=2)[:, :, 0, :],
                                        -1e30))
                                    p.step("vector", fns, s_v, writes=[R_ps_sc])
                                fns = []
                                for hh in range(4):
                                    h = 4 * hg + hh
                                    blks = []
                                    if ii > 0:
                                        pg, pt = (ii - 1) // 4, (ii - 1) % 4
                                        blks.append((0, pg % 2, pt))
                                    blks.append((1, g % 2, igi))
                                    fns.append(lambda e, h=h, igi=igi:
                                               TE.ldweights(
                                                   qaG[:, h // 2,
                                                       igi * 128:(igi + 1) * 128]))
                                    for (b, ksl, kt) in blks:
                                        fns.append(lambda e, h=h, hh=hh, b=b,
                                                   igi=igi, ksl=ksl, kt=kt:
                                                   TE.matmul(
                                                       psG[:, hh * 256 + b * 128:
                                                           hh * 256 + (b + 1) * 128],
                                                       qaG[:, h // 2,
                                                           igi * 128:(igi + 1) * 128],
                                                       kaG[:, ksl, h,
                                                           kt * 128:(kt + 1) * 128],
                                                       start=True, stop=True))
                                kread = [R_kaG[g % 2], R_qaG]
                                if ii > 0:
                                    kread.append(R_kaG[((ii - 1) // 4) % 2])
                                p.step("tensor", fns, s_t, reads=kread, writes=[R_ps_sc])
                                if G4 == "noexp":
                                    continue
                                p.step("scalar", lambda e, isl=isl, hg=hg: S.activation(
                                    aeG[:, 0, 4 * hg:4 * hg + 4, :],
                                    psG[:, 0:1024].rearrange("p (h c) -> p h c", c=256),
                                    AF.Exp, scale=0.125),
                                    s_sc, reads=[R_ps_sc], writes=[R_aeG[0]])
                            if G_PHASE < 5:
                                continue
                            mi = 0 if ii == 0 else 1
                            p.step("vector", lambda e, mi=mi: V.tensor_tensor(
                                aeG[:, 0], aeG[:, 0],
                                maskG[:, mi, :].unsqueeze(1).broadcast_to([128, 16, 256]),
                                ALU.mult),
                                s_v, reads=[R_wG], writes=[R_aeG[0]])
                            p.step("vector", lambda e, isl=isl: V.tensor_reduce(
                                denG[:, isl], aeG[:, 0], AX.X, ALU.add),
                                s_v, reads=[R_aeG[0]], writes=[R_denG[isl]])
                            p.step("vector", lambda e, isl=isl: V.reciprocal(
                                denG[:, isl], denG[:, isl]), s_v, writes=[R_denG[isl]])
                            p.step("vector", lambda e, isl=isl: V.tensor_tensor(
                                aeG[:, 0], aeG[:, 0],
                                denG[:, isl].unsqueeze(2).broadcast_to([128, 16, 256]),
                                ALU.mult),
                                s_v, reads=[R_denG[isl]], writes=[R_aeG[0]])
                            at_flat = atG[:, 0].rearrange("p h b c -> p (h b c)")
                            for rr in range(2):
                                pairs = [(h, b) for h in range(8 * rr, 8 * rr + 8)
                                         for b in range(2)]
                                p.step("tensor",
                                       [lambda e, h=h, b=b, ix=ix: TE.transpose(
                                           psGT[:, ix * 128:(ix + 1) * 128],
                                           aeG[:, 0, h, b * 128:(b + 1) * 128],
                                           eye_sb[:])
                                        for ix, (h, b) in enumerate(pairs)],
                                       s_t, reads=[R_aeG[0], R_consts],
                                       writes=[R_psGT])
                                p.step("vector", lambda e, rr=rr: V.tensor_copy(
                                    at_flat[:, rr * 2048:(rr + 1) * 2048],
                                    psGT[:, 0:2048]),
                                    s_v, reads=[R_psGT], writes=[R_atG[0]])
                            if G_PHASE < 6:
                                continue
                            fns = []
                            for h in range(16):
                                blks = []
                                if ii > 0:
                                    blks.append((0, (ii - 1) % 5))
                                blks.append((1, ii % 5))
                                for (b, vs5) in blks:
                                    fns.append(lambda e, h=h, b=b:
                                               TE.ldweights(atG[:, 0, h, b, :]))
                                    fns.append(lambda e, h=h, b=b, vs5=vs5,
                                               st0=(0 if ii > 0 else 1):
                                               TE.matmul(
                                                   psG[:, 1024 + h * 64:1024 + (h + 1) * 64],
                                                   atG[:, 0, h, b, :],
                                                   vaG[:, vs5, h * 64:(h + 1) * 64],
                                                   start=(b == st0),
                                                   stop=(b == 1)))
                            vread = [R_atG[0], R_vaG[ii % 5]]
                            if ii > 0:
                                vread.append(R_vaG[(ii - 1) % 5])
                            p.step("tensor", fns, s_t, reads=vread, writes=[R_ps_oG])
                            if G_PHASE < 7:
                                continue
                            p.step("vector", lambda e: V.tensor_copy(
                                otmG[:, 0, :], psG[:, 1024:2048]),
                                s_v, reads=[R_ps_oG], writes=[R_otmG[0]])
                            p.step("tensor",
                                   [lambda e, k=k: TE.transpose(
                                       psGT[:, k * 128:(k + 1) * 128],
                                       otmG[:, 0, k * 128:(k + 1) * 128], eye_sb[:])
                                    for k in range(8)],
                                   s_t, reads=[R_otmG[0], R_consts], writes=[R_psGT])
                            p.step("vector", lambda e: V.tensor_copy(
                                ofmG[:, 0, :, :],
                                psGT[:, 0:1024].rearrange("p (a b) -> p a b", b=128)),
                                s_v, reads=[R_psGT], writes=[R_ofmG[0]])
                            fns = []
                            for k in range(8):
                                fns.append(lambda e, k=k: TE.ldweights(
                                    ofmG[:, 0, k, :]))
                                for half in range(2):
                                    fns.append(lambda e, k=k, half=half: TE.matmul(
                                        psG[:, 2048 + half * 512:2048 + (half + 1) * 512],
                                        ofmG[:, 0, k, :],
                                        wpa[:, k, half * 512:(half + 1) * 512],
                                        start=(k == 0), stop=False))
                            fns.append(lambda e: TE.ldweights(ones_sb[0:1, :]))
                            for half in range(2):
                                fns.append(lambda e, half=half: TE.matmul(
                                    psG[:, 2048 + half * 512:2048 + (half + 1) * 512],
                                    ones_sb[0:1, :], browG[0:1, D + half * 512:D + (half + 1) * 512],
                                    start=False, stop=True))
                            p.step("tensor", fns, s_t,
                                   reads=[R_ofmG[0], R_wG, R_consts], writes=[R_ps_qkv])
                            p.step("vector", lambda e: V.tensor_copy(
                                mixG[:, 0, :], psG[:, 2048:3072]),
                                s_v, reads=[R_ps_qkv], writes=[R_mixG[0]])
                            p.step("sync", lambda e, ii=ii: e.dma_start(
                                out=mix_h[ii * 128:(ii + 1) * 128, :], in_=mixG[:, 0, :]),
                                s_st, reads=[R_mixG[0]], writes=[R_mixh[ii]], dma=True)
            p.barrier(ALL)

        if stages >= 5:
            with contextlib.ExitStack() as ist:
                wqs = ist.enter_context(nc.sbuf_tensor("i_wqs", [128, 8, D], BF16))
                wks = ist.enter_context(nc.sbuf_tensor("i_wks", [128, 8, D], BF16))
                wvs = ist.enter_context(nc.sbuf_tensor("i_wvs", [128, 8, D], BF16))
                wos = ist.enter_context(nc.sbuf_tensor("i_wos", [128, 8, D], BF16))
                browI = ist.enter_context(nc.sbuf_tensor("i_brow", [1, 2 * D], BF16))
                bqkI = ist.enter_context(nc.sbuf_tensor("i_bqk", [128, 16], F32))
                trilI = ist.enter_context(nc.sbuf_tensor("i_tril", [128, 128], BF16))
                xninI = ist.enter_context(nc.sbuf_tensor("i_xnin", [128, 8, 4, 128], BF16))
                r1I = ist.enter_context(nc.sbuf_tensor("i_r1", [128, 2, D], F32))
                utmI = ist.enter_context(nc.sbuf_tensor("i_utm", [128, D], BF16))
                ufmI = ist.enter_context(nc.sbuf_tensor("i_ufm", [128, 8, 128], BF16))
                qhat = ist.enter_context(nc.sbuf_tensor("i_qhat", [128, 16, 512], BF16))
                khat = ist.enter_context(nc.sbuf_tensor("i_khat", [128, 16, 512], BF16))
                m_sb = ist.enter_context(nc.sbuf_tensor("i_m", [128, 8, 512], F32))
                r_sb = ist.enter_context(nc.sbuf_tensor("i_r", [128, 8, 512], F32))
                ktmI = ist.enter_context(nc.sbuf_tensor("i_ktm", [128, 16, 128], BF16))
                vextI = ist.enter_context(nc.sbuf_tensor("i_vext", [128, 4, 16, 65], BF16))
                SsbI = ist.enter_context(nc.sbuf_tensor("i_Ssb", [128, 8, 65], BF16))
                SfI = ist.enter_context(nc.sbuf_tensor("i_Sf", [128, 8, 65], F32))
                atI = ist.enter_context(nc.sbuf_tensor("i_at", [128, 128], BF16))
                denI = ist.enter_context(nc.sbuf_tensor("i_den", [128, 16], F32))
                otmI = ist.enter_context(nc.sbuf_tensor("i_otm", [128, D], BF16))
                ofmI = ist.enter_context(nc.sbuf_tensor("i_ofm", [128, 8, 128], BF16))
                ssmI = ist.enter_context(nc.sbuf_tensor("i_ssm", [128, 2, D], F32))
                psI = ist.enter_context(nc.psum_tensor("i_ps", [128, 4096], F32))
                col_S = lambda pp: 72 * pp if pp < 7 else 512
                col_o = lambda h: (1024 + 72 * h if h < 7 else
                                   1536 + 72 * (h - 7) if h < 14 else
                                   2048 + 72 * (h - 14))
                R_wI = Res("wI")
                R_xninI = [Res(f"xninI{j}") for j in range(4)]
                R_r1I = [Res(f"r1I{j}") for j in range(2)]
                R_utmI, R_ufmI = Res("utmI"), Res("ufmI")
                R_qhat, R_khat = Res("qhat"), Res("khat")
                R_mr = Res("mrI")
                R_ktm = Res("ktmI")
                R_vext = [Res(f"vextI{j}") for j in range(4)]
                R_Ssb = Res("SsbI")
                R_Sf = Res("SfI")
                R_atI = Res("atI")
                R_denI = Res("denI")
                R_otmI, R_ofmI = Res("otmI"), Res("ofmI")
                R_ssmI = [Res(f"ssmI{j}") for j in range(2)]
                R_ps_S = Res("ps_S")
                R_ps_po = Res("ps_po")
                R_ps_at = Res("ps_atI")
                R_ps_qI = Res("ps_qI")

                p.step("gpsimd", lambda e: e.dma_start(
                    out=wqs[:], in_=wqkv_s_h.rearrange("(k p) n -> p k n", p=128)[:, :, 0:D]),
                    s_w, writes=[R_wI], dma=True)
                p.step("gpsimd", lambda e: e.dma_start(
                    out=wks[:], in_=wqkv_s_h.rearrange("(k p) n -> p k n", p=128)[:, :, D:2 * D]),
                    s_w, writes=[R_wI], dma=True)
                p.step("gpsimd", lambda e: e.dma_start(
                    out=wvs[:], in_=wqkv_s_h.rearrange("(k p) n -> p k n", p=128)[:, :, 2 * D:3 * D]),
                    s_w, writes=[R_wI], dma=True)
                p.step("gpsimd", lambda e: e.dma_start(
                    out=wos[:], in_=wo_s_h.rearrange("(k p) n -> p k n", p=128)),
                    s_w, writes=[R_wI], dma=True)
                p.step("gpsimd", lambda e: e.dma_start(out=browI[0:1, 0:D], in_=bv_s_h[:, :]),
                       s_w, writes=[R_wI], dma=True)
                p.step("gpsimd", lambda e: e.dma_start(out=browI[0:1, D:2 * D], in_=bo_s_h[:, :]),
                       s_w, writes=[R_wI], dma=True)
                p.step("sync", lambda e: e.dma_start(out=bqkI[:, 0:8], in_=bq_s_h[:, :]),
                       s_w, writes=[R_wI], dma=True)
                p.step("sync", lambda e: e.dma_start(out=bqkI[:, 8:16], in_=bk_s_h[:, :]),
                       s_w, writes=[R_wI], dma=True)
                p.step("sync", lambda e: e.dma_start(out=trilI[:], in_=tril_h[:, :]),
                       s_w, writes=[R_wI], dma=True)
                p.step("vector", lambda e: V.memset(qhat[:], 0.0), s_v, writes=[R_qhat])
                p.step("vector", lambda e: V.memset(khat[:], 0.0), s_v, writes=[R_khat])
                p.step("vector", lambda e: V.memset(
                    vextI[:].rearrange("p a h c -> p (a h) c")[:, :, 64:65], 1.0),
                    s_v, writes=[R_vext[0], R_vext[1], R_vext[2], R_vext[3]])

                ELU_D = float(np.float32(1.0001)) - 1.0

                for g in range(8):
                    for j in range(4):
                        i = 4 * g + j
                        sl = i % 2
                        p._wait("scalar", s_t, s_t.n)
                        p.step("sync", lambda e, i=i, sl=sl: e.dma_start(
                            out=r1I[:, sl, :], in_=r1_h[i * 128:(i + 1) * 128, :]),
                            s_ld, reads=[R_r1h[i]], writes=[R_r1I[sl]], dma=True)
                        p.step("vector", lambda e, i=i, sl=sl: V.tensor_scalar(
                            utmI[:, :], r1I[:, sl, :], stat[:, M1 + i:M1 + i + 1],
                            stat2[:, i:i + 1], ALU.subtract, ALU.mult),
                            s_v, reads=[R_r1I[sl], R_stat, R_stat2], writes=[R_utmI])
                        p.step_dmat(_dmq("I"),
                               [lambda e, k=k: e.dma_start_transpose(
                                   ufmI[:, k, :], utmI[:, k * 128:(k + 1) * 128])
                                for k in range(8)],
                               s_dt, reads=[R_utmI], writes=[R_ufmI], dma=True)
                        p.step("vector",
                               [lambda e, j=j, k=k: V.tensor_scalar(
                                   xninI[:, k, j, :], ufmI[:, k, :],
                                   gb1[:, k:k + 1], gb1[:, 8 + k:8 + k + 1],
                                   ALU.mult, ALU.add)
                                for k in range(8)],
                               s_v, reads=[R_ufmI, R_gb1], writes=[R_xninI[j]])
                        fns = []
                        for k in range(8):
                            fns.append(lambda e, j=j, k=k: TE.ldweights(
                                xninI[:, k, j, :]))
                            for half in range(2):
                                fns.append(lambda e, j=j, k=k, half=half: TE.matmul(
                                    psI[:, 3072 + half * 512:3072 + (half + 1) * 512],
                                    xninI[:, k, j, :],
                                    wvs[:, k, half * 512:(half + 1) * 512],
                                    start=(k == 0), stop=False))
                        fns.append(lambda e: TE.ldweights(ones_sb[0:1, :]))
                        for half in range(2):
                            fns.append(lambda e, half=half: TE.matmul(
                                psI[:, 3072 + half * 512:3072 + (half + 1) * 512],
                                ones_sb[0:1, :], browI[0:1, half * 512:(half + 1) * 512],
                                start=False, stop=True))
                        p.step("tensor", fns, s_t,
                               reads=[R_xninI[j], R_wI, R_consts], writes=[R_ps_qI])
                        p.step("vector", lambda e, j=j: V.tensor_copy(
                            vextI[:, j, :, 0:64],
                            psI[:, 3072:4096].rearrange("p (h c) -> p h c", c=64)),
                            s_v, reads=[R_ps_qI], writes=[R_vext[j]])

                    for which in ("q", "k"):
                        wmat = wqs if which == "q" else wks
                        scl = 0.125 if which == "q" else 1.0
                        bof = 0 if which == "q" else 8
                        dstp = qhat if which == "q" else khat
                        Rdst = R_qhat if which == "q" else R_khat
                        for dc in range(8):
                            fns = []
                            for k in range(8):
                                fns.append(lambda e, k=k, dc=dc, wmat=wmat:
                                           TE.ldweights(wmat[:, k, dc * 128:(dc + 1) * 128]))
                                fns.append(lambda e, k=k, dc=dc, wmat=wmat:
                                           TE.matmul(
                                               psI[:, 3072:3584],
                                               wmat[:, k, dc * 128:(dc + 1) * 128],
                                               xninI[:, k, :, :].rearrange("p a b -> p (a b)"),
                                               start=(k == 0), stop=(k == 7)))
                            p.step("tensor", fns, s_t,
                                   reads=[R_wI] + [R_xninI[j] for j in range(4)],
                                   writes=[R_ps_qI])
                            p.step("vector", lambda e, dc=dc, bof=bof:
                                   V.tensor_scalar(
                                       m_sb[:, dc, :], psI[:, 3072:3584],
                                       bqkI[:, bof + dc:bof + dc + 1], None, ALU.add),
                                   s_v, reads=[R_ps_qI, R_wI], writes=[R_mr])
                        p.step("vector", lambda e, scl=scl: V.tensor_scalar(
                            r_sb[:], m_sb[:], scl, 0.0, ALU.mult, ALU.max),
                            s_v, reads=[R_mr], writes=[R_mr])
                        p.step("vector", lambda e, scl=scl: V.tensor_scalar(
                            m_sb[:], m_sb[:], scl, 0.0, ALU.mult, ALU.min),
                            s_v, writes=[R_mr])
                        p.step("scalar", lambda e: S.activation(
                            m_sb[:], m_sb[:], AF.Exp), s_sc, writes=[R_mr])
                        p.step("vector", lambda e, dstp=dstp: V.scalar_tensor_tensor(
                            dstp[0:64, 0:16:2, :],
                            m_sb[0:64, :, :], ELU_D, r_sb[0:64, :, :],
                            ALU.add, ALU.add),
                            s_v, reads=[R_mr], writes=[Rdst])
                        p.step("vector", lambda e, dstp=dstp: V.scalar_tensor_tensor(
                            dstp[64:128, 1:16:2, :],
                            m_sb[64:128, :, :], ELU_D, r_sb[64:128, :, :],
                            ALU.add, ALU.add),
                            s_v, reads=[R_mr], writes=[Rdst])

                    for j in range(4):
                        ii = 4 * g + j
                        sl = ii % 2
                        p._wait("scalar", s_t, s_t.n)
                        p.step_dmat(_dmq("I"),
                               [lambda e, j=j, h=h: e.dma_start_transpose(
                                   ktmI[:, h, :], khat[:, h, j * 128:(j + 1) * 128])
                                for h in range(16)],
                               s_dt, reads=[R_khat], writes=[R_ktm], dma=True)
                        if ii == 0:
                            p.step("vector", lambda e: V.memset(SfI[:], 0.0),
                                   s_v, writes=[R_Sf])
                            p.step("vector", lambda e: V.memset(SsbI[:], 0.0),
                                   s_v, writes=[R_Ssb])
                        else:
                            p.step("vector", lambda e: V.tensor_copy(SsbI[:], SfI[:]),
                                   s_v, reads=[R_Sf], writes=[R_Ssb])
                        for h in range(16):
                            p.step("tensor",
                                   [lambda e, j=j, h=h: TE.ldweights(
                                       khat[:, h, j * 128:(j + 1) * 128]),
                                    lambda e, j=j, h=h: TE.matmul(
                                        psI[:, 2560:2688],
                                        khat[:, h, j * 128:(j + 1) * 128],
                                        qhat[:, h, j * 128:(j + 1) * 128],
                                        start=True, stop=True)],
                                   s_t, reads=[R_khat, R_qhat], writes=[R_ps_at])
                            p.step("vector", lambda e: V.tensor_tensor(
                                atI[:], psI[:, 2560:2688], trilI[:], ALU.mult),
                                s_v, reads=[R_ps_at, R_wI], writes=[R_atI])
                            p.step("tensor",
                                   [lambda e, j=j, h=h: TE.ldweights(
                                       qhat[:, h, j * 128:(j + 1) * 128]),
                                    lambda e, j=j, h=h: TE.matmul(
                                        psI[:, col_o(h):col_o(h) + 65],
                                        qhat[:, h, j * 128:(j + 1) * 128],
                                        SsbI[:, h // 2, :],
                                        start=True, stop=False),
                                    lambda e, h=h: TE.ldweights(atI[:]),
                                    lambda e, h=h, j=j: TE.matmul(
                                        psI[:, col_o(h):col_o(h) + 65],
                                        atI[:], vextI[:, j, h, :],
                                        start=False, stop=True)],
                                   s_t, reads=[R_qhat, R_Ssb, R_atI, R_vext[j]],
                                   writes=[R_ps_po])
                            tp = None if h % 2 == 0 else (0, 64)
                            rows = slice(0, 64) if h % 2 == 0 else slice(64, 128)
                            p.step("tensor",
                                   [lambda e, h=h, tp=tp: TE.ldweights(
                                       ktmI[:, h, (h % 2) * 64:(h % 2) * 64 + 64],
                                       tile_position=tp),
                                    lambda e, h=h, j=j, tp=tp, rows=rows:
                                    TE.matmul(
                                        psI[rows, col_S(h // 2):col_S(h // 2) + 65],
                                        ktmI[:, h, (h % 2) * 64:(h % 2) * 64 + 64],
                                        vextI[:, j, h, :],
                                        start=True, stop=True,
                                        tile_position=tp)],
                                   s_t, reads=[R_ktm, R_vext[j]], writes=[R_ps_S])
                        p.step("vector", lambda e: V.tensor_tensor(
                            SfI[:, 0:7, :],
                            psI[:, 0:504].rearrange("p (a b) -> p a b", b=72)[:, :, 0:65],
                            SfI[:, 0:7, :], ALU.add),
                            s_v, reads=[R_ps_S], writes=[R_Sf])
                        p.step("vector", lambda e: V.tensor_tensor(
                            SfI[:, 7, :], psI[:, 512:577], SfI[:, 7, :], ALU.add),
                            s_v, reads=[R_ps_S], writes=[R_Sf])
                        p.step("vector", lambda e: V.tensor_copy(
                            denI[:, 0:7],
                            psI[:, 1024:1528].rearrange(
                                "p (a b) -> p a b", b=72)[:, 0:7, 64:65].squeeze(2)),
                            s_v, reads=[R_ps_po], writes=[R_denI])
                        p.step("vector", lambda e: V.tensor_copy(
                            denI[:, 7:14],
                            psI[:, 1536:2040].rearrange(
                                "p (a b) -> p a b", b=72)[:, 0:7, 64:65].squeeze(2)),
                            s_v, reads=[R_ps_po], writes=[R_denI])
                        p.step("vector", lambda e: V.tensor_copy(
                            denI[:, 14:16],
                            psI[:, 2048:2192].rearrange(
                                "p (a b) -> p a b", b=72)[:, 0:2, 64:65].squeeze(2)),
                            s_v, reads=[R_ps_po], writes=[R_denI])
                        p.step("vector", lambda e: V.tensor_scalar(
                            denI[:], denI[:], 1e-6, None, ALU.add),
                            s_v, writes=[R_denI])
                        p.step("vector", lambda e: V.reciprocal(denI[:], denI[:]),
                               s_v, writes=[R_denI])
                        p.step("vector",
                               [lambda e, h=h: V.tensor_scalar(
                                   otmI[:, h * 64:(h + 1) * 64],
                                   psI[:, col_o(h):col_o(h) + 64],
                                   denI[:, h:h + 1], None, ALU.mult)
                                for h in range(16)],
                               s_v, reads=[R_ps_po, R_denI], writes=[R_otmI])
                        p.step_dmat(_dmq("I"),
                               [lambda e, k=k: e.dma_start_transpose(
                                   ofmI[:, k, :], otmI[:, k * 128:(k + 1) * 128])
                                for k in range(8)],
                               s_dt, reads=[R_otmI], writes=[R_ofmI], dma=True)
                        fns = []
                        for k in range(8):
                            fns.append(lambda e, k=k: TE.ldweights(ofmI[:, k, :]))
                            for half in range(2):
                                fns.append(lambda e, k=k, half=half: TE.matmul(
                                    psI[:, 3072 + half * 512:3072 + (half + 1) * 512],
                                    ofmI[:, k, :], wos[:, k, half * 512:(half + 1) * 512],
                                    start=(k == 0), stop=False))
                        fns.append(lambda e: TE.ldweights(ones_sb[0:1, :]))
                        for half in range(2):
                            fns.append(lambda e, half=half: TE.matmul(
                                psI[:, 3072 + half * 512:3072 + (half + 1) * 512],
                                ones_sb[0:1, :], browI[0:1, D + half * 512:D + (half + 1) * 512],
                                start=False, stop=True))
                        p.step("tensor", fns, s_t,
                               reads=[R_ofmI, R_wI, R_consts], writes=[R_ps_qI])
                        p.step("vector", lambda e, sl=sl: V.tensor_copy(
                            ssmI[:, sl, :], psI[:, 3072:4096]),
                            s_v, reads=[R_ps_qI], writes=[R_ssmI[sl]])
                        p.step("gpsimd", lambda e, ii=ii, sl=sl: e.dma_start(
                            out=mix_h[ii * 128:(ii + 1) * 128, :], in_=ssmI[:, sl, :],
                            accum_op=ALU.add),
                            s_st, reads=[R_ssmI[sl]], writes=[R_mixh[ii]], dma=True)
                if dbg:
                    d_qh = dout("d_qh", (128, 16 * 512))
                    p.step("gpsimd", lambda e: e.dma_start(
                        out=d_qh[:, :], in_=qhat[:].rearrange("p h n -> p (h n)")),
                        s_st, reads=[R_qhat], writes=[Res("dqh")], dma=True)
                    d_kh = dout("d_kh", (128, 16 * 512))
                    p.step("gpsimd", lambda e: e.dma_start(
                        out=d_kh[:, :], in_=khat[:].rearrange("p h n -> p (h n)")),
                        s_st, reads=[R_khat], writes=[Res("dkh")], dma=True)
                    d_Ssb = dout("d_Ssb", (128, 8 * 65))
                    p.step("gpsimd", lambda e: e.dma_start(
                        out=d_Ssb[:, :], in_=SsbI[:].rearrange("p a c -> p (a c)")),
                        s_st, reads=[R_Ssb], writes=[Res("dSsb")], dma=True)
                    d_oI = dout("d_oI", (128, D))
                    p.step("gpsimd", lambda e: e.dma_start(out=d_oI[:, :], in_=otmI[:]),
                        s_st, reads=[R_otmI], writes=[Res("doI")], dma=True)
            p.barrier(ALL)

        R_outh = [Res(f"outh{i}") for i in range(NT)]
        if stages >= 6:
            with contextlib.ExitStack() as jst:
                wjo = jst.enter_context(nc.sbuf_tensor("j_wo", [128, 8, D], BF16))
                w1J = jst.enter_context(nc.sbuf_tensor("j_w1", [128, 8, 4 * D], BF16))
                w2J = jst.enter_context(nc.sbuf_tensor("j_w2", [128, 32, D], BF16))
                browJ = jst.enter_context(nc.sbuf_tensor("j_brow", [1, 6 * D], BF16))
                gbm = jst.enter_context(nc.sbuf_tensor("j_gbm", [128, 16], F32))
                mixbf = jst.enter_context(nc.sbuf_tensor("j_mixbf", [128, 1, D], BF16))
                mixfm = jst.enter_context(nc.sbuf_tensor("j_mixfm", [128, 8, 128], BF16))
                r1J = jst.enter_context(nc.sbuf_tensor("j_r1", [128, 2, D], F32))
                xn1J = jst.enter_context(nc.sbuf_tensor("j_xn1", [128, D], F32))
                y1J = jst.enter_context(nc.sbuf_tensor("j_y1", [128, 2, D], F32))
                sumsJ = jst.enter_context(nc.sbuf_tensor("j_sums", [128, 8], F32))
                htmJ = jst.enter_context(nc.sbuf_tensor("j_htm", [128, D], BF16))
                hfmJ = jst.enter_context(nc.sbuf_tensor("j_hfm", [128, 8, 128], BF16))
                geltJ = jst.enter_context(nc.sbuf_tensor("j_gelt", [128, 2 * D], BF16))
                gelfJ = jst.enter_context(nc.sbuf_tensor("j_gelf", [128, 32, 128], BF16))
                outJ = jst.enter_context(nc.sbuf_tensor("j_out", [128, 1, D], F32))
                trJ = jst.enter_context(nc.sbuf_tensor("j_tr", [128, D], BF16))
                psJ = jst.enter_context(nc.psum_tensor("j_ps", [128, 3072], F32))
                psJT = jst.enter_context(nc.psum_tensor("j_psT", [128, 2048], BF16))
                R_wJ = Res("wJ")
                R_mixbf = [Res("mixbf0")]
                R_mixfm = Res("mixfm")
                R_r1J = [Res(f"r1J{j}") for j in range(2)]
                R_xn1J = Res("xn1J")
                R_y1J = [Res(f"y1J{j}") for j in range(2)]
                R_sumsJ = Res("sumsJ")
                R_htmJ, R_hfmJ = Res("htmJ"), Res("hfmJ")
                R_geltJ, R_gelfJ = Res("geltJ"), Res("gelfJ")
                R_outJ = [Res("outJ0")]
                R_trJ = Res("trJ")
                R_ps_mx = Res("ps_mx")
                R_ps_w1 = Res("ps_w1")
                R_ps_ml = Res("ps_ml")
                R_ps_tr = Res("ps_trJ")
                R_psJT = R_ps_tr

                p.step("gpsimd", lambda e: e.dma_start(
                    out=wjo[:], in_=wo1_h.rearrange("(k p) n -> p k n", p=128)),
                    s_w, writes=[R_wJ], dma=True)
                p.step("gpsimd", lambda e: e.dma_start(
                    out=w1J[:], in_=w1_h.rearrange("(k p) n -> p k n", p=128)),
                    s_w, writes=[R_wJ], dma=True)
                p.step("gpsimd", lambda e: e.dma_start(
                    out=w2J[:], in_=w2_h.rearrange("(k p) n -> p k n", p=128)),
                    s_w, writes=[R_wJ], dma=True)
                p.step("gpsimd", lambda e: e.dma_start(out=browJ[0:1, 0:D], in_=bo1_h[:, :]),
                       s_w, writes=[R_wJ], dma=True)
                p.step("gpsimd", lambda e: e.dma_start(out=browJ[0:1, D:5 * D], in_=b1m_h[:, :]),
                       s_w, writes=[R_wJ], dma=True)
                p.step("gpsimd", lambda e: e.dma_start(out=browJ[0:1, 5 * D:6 * D], in_=b2m_h[:, :]),
                       s_w, writes=[R_wJ], dma=True)
                p.step("sync", lambda e: e.dma_start(out=gbm[:, 0:8], in_=gm_h[:, :]),
                       s_w, writes=[R_wJ], dma=True)
                p.step("sync", lambda e: e.dma_start(out=gbm[:, 8:16], in_=bm_h[:, :]),
                       s_w, writes=[R_wJ], dma=True)

                for i in range(int(os.environ.get('J_NT', NT))):
                    sl = i % 2
                    p.step("gpsimd", lambda e, i=i: e.dma_start(
                        out=mixbf[:, 0, :], in_=mix_h[i * 128:(i + 1) * 128, :]),
                        s_ld, reads=[R_mixh[i]], writes=[R_mixbf[0]], dma=True)
                    p.step_dmat(_dmq("J"),
                           [lambda e, k=k: e.dma_start_transpose(
                               mixfm[:, k, :], mixbf[:, 0, k * 128:(k + 1) * 128])
                            for k in range(8)],
                           s_dt, reads=[R_mixbf[0]], writes=[R_mixfm], dma=True)
                    p.step("sync", lambda e, i=i, sl=sl: e.dma_start(
                        out=r1J[:, sl, :], in_=r1_h[i * 128:(i + 1) * 128, :]),
                        s_ld, reads=[R_r1h[i]], writes=[R_r1J[sl]], dma=True)
                    fns = []
                    for k in range(8):
                        fns.append(lambda e, k=k: TE.ldweights(mixfm[:, k, :]))
                        for half in range(2):
                            fns.append(lambda e, k=k, half=half: TE.matmul(
                                psJ[:, half * 512:(half + 1) * 512],
                                mixfm[:, k, :], wjo[:, k, half * 512:(half + 1) * 512],
                                start=(k == 0), stop=False))
                    fns.append(lambda e: TE.ldweights(ones_sb[0:1, :]))
                    for half in range(2):
                        fns.append(lambda e, half=half: TE.matmul(
                            psJ[:, half * 512:(half + 1) * 512],
                            ones_sb[0:1, :], browJ[0:1, half * 512:(half + 1) * 512],
                            start=False, stop=True))
                    p.step("tensor", fns, s_t,
                           reads=[R_mixfm, R_wJ, R_consts], writes=[R_ps_mx])
                    if J_PHASE < 2:
                        continue
                    p.step("vector", lambda e, i=i, sl=sl: V.tensor_scalar(
                        xn1J[:], r1J[:, sl, :], stat[:, M1 + i:M1 + i + 1],
                        stat[:, RSTD1 + i:RSTD1 + i + 1], ALU.subtract, ALU.mult),
                        s_v, reads=[R_r1J[sl], R_stat], writes=[R_xn1J])
                    p.step("vector", lambda e, sl=sl: V.tensor_tensor(
                        y1J[:, sl, :], psJ[:, 0:1024], xn1J[:], ALU.add),
                        s_v, reads=[R_ps_mx, R_xn1J], writes=[R_y1J[sl]])
                    p.step("scalar", lambda e, sl=sl: S.activation(
                        trJ[:], y1J[:, sl, :], AF.Square, accum_out=sumsJ[:, 0:1]),
                        s_sc, reads=[R_y1J[sl]], writes=[R_trJ, R_sumsJ])
                    p.step("vector", lambda e, sl=sl: V.tensor_scalar(
                        trJ[:], y1J[:, sl, :], 1.0, 0.0, ALU.mult, ALU.add,
                        accum_out=sumsJ[:, 1:2]),
                        s_v, reads=[R_y1J[sl]], writes=[R_trJ, R_sumsJ])
                    p.step("vector", lambda e: V.tensor_scalar(
                        sumsJ[:, 4:5], sumsJ[:, 1:2], 1.0 / D, None, ALU.mult),
                        s_v, writes=[R_sumsJ])
                    p.step("vector", lambda e: V.tensor_scalar(
                        sumsJ[:, 2:3], sumsJ[:, 4:5], sumsJ[:, 4:5], None, ALU.mult),
                        s_v, writes=[R_sumsJ])
                    p.step("vector", lambda e: V.scalar_tensor_tensor(
                        sumsJ[:, 2:3], sumsJ[:, 0:1], 1.0 / D, sumsJ[:, 2:3],
                        ALU.mult, ALU.subtract), s_v, writes=[R_sumsJ])
                    p.step("vector", lambda e: V.tensor_scalar(
                        sumsJ[:, 2:3], sumsJ[:, 2:3], 1e-5, None, ALU.add),
                        s_v, writes=[R_sumsJ])
                    p.step("vector", lambda e: V.reciprocal(
                        sumsJ[:, 2:3], sumsJ[:, 2:3]), s_v, writes=[R_sumsJ])
                    p.step("scalar", lambda e: S.activation(
                        sumsJ[:, 5:6], sumsJ[:, 2:3], AF.Sqrt),
                        s_sc, writes=[R_sumsJ])
                    if J_PHASE < 3:
                        continue
                    p.step("vector", lambda e, sl=sl: V.tensor_scalar(
                        htmJ[:], y1J[:, sl, :], sumsJ[:, 4:5], sumsJ[:, 5:6],
                        ALU.subtract, ALU.mult),
                        s_v, reads=[R_y1J[sl], R_sumsJ], writes=[R_htmJ])
                    p.step("tensor",
                           [lambda e, k=k: TE.transpose(
                               psJT[:, k * 128:(k + 1) * 128],
                               htmJ[:, k * 128:(k + 1) * 128], eye_sb[:])
                            for k in range(8)],
                           s_t, reads=[R_htmJ, R_consts], writes=[R_psJT])
                    p.step("vector", lambda e: V.tensor_copy(
                        hfmJ[:].rearrange("p a b -> p (a b)"), psJT[:, 0:1024]),
                        s_v, reads=[R_psJT], writes=[R_hfmJ])
                    p.step("vector",
                           [lambda e, k=k: V.tensor_scalar(
                               hfmJ[:, k, :], hfmJ[:, k, :],
                               gbm[:, k:k + 1], gbm[:, 8 + k:8 + k + 1],
                               ALU.mult, ALU.add)
                            for k in range(8)],
                           s_v, reads=[R_wJ], writes=[R_hfmJ])
                    if J_PHASE < 4:
                        continue
                    R_psJT = R_ps_tr
                    for qp in range(4):
                        fns = []
                        for k in range(8):
                            fns.append(lambda e, k=k: TE.ldweights(hfmJ[:, k, :]))
                            for q2 in range(2):
                                fns.append(lambda e, k=k, q2=q2, qp=qp: TE.matmul(
                                    psJ[:, 1024 + q2 * 512:1024 + (q2 + 1) * 512],
                                    hfmJ[:, k, :],
                                    w1J[:, k, qp * 1024 + q2 * 512:qp * 1024 + (q2 + 1) * 512],
                                    start=(k == 0), stop=False))
                        fns.append(lambda e: TE.ldweights(ones_sb[0:1, :]))
                        for q2 in range(2):
                            fns.append(lambda e, q2=q2, qp=qp: TE.matmul(
                                psJ[:, 1024 + q2 * 512:1024 + (q2 + 1) * 512],
                                ones_sb[0:1, :],
                                browJ[0:1, D + qp * 1024 + q2 * 512:D + qp * 1024 + (q2 + 1) * 512],
                                start=False, stop=True))
                        p.step("tensor", fns, s_t,
                               reads=[R_hfmJ, R_wJ, R_consts], writes=[R_ps_w1])
                        p._wait("scalar", s_t, s_t.n)
                        p.step("scalar", lambda e, qp=qp: S.activation(
                            geltJ[:, (qp % 2) * 1024:(qp % 2) * 1024 + 1024],
                            psJ[:, 1024:2048], AF.Gelu),
                            s_sc, reads=[R_ps_w1], writes=[R_geltJ])
                        p.step("vector", lambda e, qp=qp: V.tensor_copy(
                            geltJ[:, (qp % 2) * 1024:(qp % 2) * 1024 + 1024],
                            geltJ[:, (qp % 2) * 1024:(qp % 2) * 1024 + 1024]),
                            s_v, writes=[R_geltJ])
                        half = qp % 2
                        p.step("tensor",
                               [lambda e, qp=qp, jj=jj, half=half:
                                TE.transpose(
                                    psJT[:, half * 1024 + jj * 128:half * 1024 + (jj + 1) * 128],
                                    geltJ[:, half * 1024 + jj * 128:half * 1024 + (jj + 1) * 128],
                                    eye_sb[:])
                                for jj in range(8)],
                               s_t, reads=[R_geltJ, R_consts], writes=[R_psJT])
                        p.step("vector", lambda e, qp=qp, half=half: V.tensor_copy(
                            gelfJ[:, qp * 8:qp * 8 + 8, :],
                            psJT[:, half * 1024:half * 1024 + 1024].rearrange(
                                "p (a b) -> p a b", b=128)),
                            s_v, reads=[R_psJT], writes=[R_gelfJ])
                    if J_PHASE < 5:
                        continue
                    fns = []
                    for ic in range(32):
                        fns.append(lambda e, ic=ic: TE.ldweights(gelfJ[:, ic, :]))
                        for half in range(2):
                            fns.append(lambda e, ic=ic, half=half: TE.matmul(
                                psJ[:, 2048 + half * 512:2048 + (half + 1) * 512],
                                gelfJ[:, ic, :], w2J[:, ic, half * 512:(half + 1) * 512],
                                start=(ic == 0), stop=False))
                    fns.append(lambda e: TE.ldweights(ones_sb[0:1, :]))
                    for half in range(2):
                        fns.append(lambda e, half=half: TE.matmul(
                            psJ[:, 2048 + half * 512:2048 + (half + 1) * 512],
                            ones_sb[0:1, :], browJ[0:1, 5 * D + half * 512:5 * D + (half + 1) * 512],
                            start=False, stop=True))
                    p.step("tensor", fns, s_t,
                           reads=[R_gelfJ, R_wJ, R_consts], writes=[R_ps_ml])
                    if J5 == 'mm':
                        continue
                    if J5 == 'noMLP':
                        p.step("vector", lambda e, sl=sl: V.tensor_copy(
                            outJ[:, 0, :], y1J[:, sl, :]),
                            s_v, reads=[R_y1J[sl]], writes=[R_outJ[0]])
                    elif J5 == 'gelf0':
                        p.step("vector", lambda e: V.tensor_copy(
                            outJ[:, 0, :], gelfJ[:, 0:8, :].rearrange("p a b -> p (a b)")),
                            s_v, reads=[R_gelfJ], writes=[R_outJ[0]])
                    elif J5 == 'gelf1':
                        p.step("vector", lambda e: V.tensor_copy(
                            outJ[:, 0, :], gelfJ[:, 16:24, :].rearrange("p a b -> p (a b)")),
                            s_v, reads=[R_gelfJ], writes=[R_outJ[0]])
                    elif J5 == 'z10':
                        p.step("vector", lambda e: V.tensor_copy(
                            outJ[:, 0, :], psJ[:, 1024:2048]),
                            s_v, reads=[R_ps_w1], writes=[R_outJ[0]])
                    elif J5 == 'gel0':
                        p.step("vector", lambda e: V.tensor_copy(
                            outJ[:, 0, :], geltJ[:, 0:1024]),
                            s_v, reads=[R_geltJ], writes=[R_outJ[0]])
                    elif J5 == 'h0':
                        p.step("vector", lambda e: V.tensor_copy(
                            outJ[:, 0, :],
                            hfmJ[:].rearrange("p a b -> p (a b)")),
                            s_v, reads=[R_hfmJ], writes=[R_outJ[0]])
                    elif J5 == 'mlpOnly':
                        p.step("vector", lambda e: V.tensor_copy(
                            outJ[:, 0, :], psJ[:, 2048:3072]),
                            s_v, reads=[R_ps_ml], writes=[R_outJ[0]])
                    else:
                        p.step("vector", lambda e, sl=sl: V.tensor_tensor(
                            outJ[:, 0, :], psJ[:, 2048:3072], y1J[:, sl, :], ALU.add),
                            s_v, reads=[R_ps_ml, R_y1J[sl]], writes=[R_outJ[0]])
                    if J5 in ('full', 'noMLP'):
                        p.step("vector", lambda e, sl=sl: V.tensor_tensor(
                            outJ[:, 0, :], outJ[:, 0, :], r1J[:, sl, :], ALU.add),
                            s_v, reads=[R_r1J[sl]], writes=[R_outJ[0]])
                    if J5 != 'add':
                        p.step("sync", lambda e, i=i: e.dma_start(
                            out=out_h[i * 128:(i + 1) * 128, :], in_=outJ[:, 0, :]),
                            s_st, reads=[R_outJ[0]], writes=[R_outh[i]], dma=True)
                    p._wait("scalar", s_t, s_t.n)
                    jb = os.environ.get('J_BARRIER')
                    if jb == '1':
                        p.barrier(ALL)
                    elif jb == 'dt':
                        p.barrier([s_dt])
                    elif jb == 't':
                        p.barrier([s_t])
                    elif jb == 'tv':
                        p._wait("vector", s_t, s_t.n)
                    elif jb == 'tsc':
                        p._wait("scalar", s_t, s_t.n)
                    elif jb == 'tt':
                        p._wait("tensor", s_t, s_t.n)
                    elif jb == 'sc':
                        p.barrier([s_sc])
                    elif jb == 'v':
                        p.barrier([s_v])
            p.barrier(ALL)

        if stages >= 7:
            with contextlib.ExitStack() as fst:
                wgq = fst.enter_context(nc.sbuf_tensor("f_wg", [128, 8, 2816], BF16))
                wuq = fst.enter_context(nc.sbuf_tensor("f_wu", [128, 8, 2816], BF16))
                wdq = fst.enter_context(nc.sbuf_tensor("f_wd", [128, 22, 1024], BF16))
                qsrc = fst.enter_context(nc.sbuf_tensor("f_qsrc", [128, 1408], F32))
                qa = fst.enter_context(nc.sbuf_tensor("f_qa", [128, 1408], F32))
                qacc = fst.enter_context(nc.sbuf_tensor("f_qacc", [128, 1408], F32))
                qb = fst.enter_context(nc.sbuf_tensor("f_qb", [128, 1408], F32))
                qbf = fst.enter_context(nc.sbuf_tensor("f_qbf", [128, 2, 1408], BF16))
                qbfX = fst.enter_context(nc.sbuf_tensor("f_qbfX", [128, 2, 1024], BF16))
                qscl = fst.enter_context(nc.sbuf_tensor("f_qscl", [128, 2 * 176], F32))
                xqfm = fst.enter_context(nc.sbuf_tensor("f_xqfm", [128, 2, 8, 128], BF16))
                hqfm = fst.enter_context(nc.sbuf_tensor("f_hqfm", [128, 2, 11, 128], BF16))
                silF = fst.enter_context(nc.sbuf_tensor("f_sil", [128, 1408], BF16))
                r1F = fst.enter_context(nc.sbuf_tensor("f_r1", [128, 2, D], F32))
                f1F = fst.enter_context(nc.sbuf_tensor("f_f1", [128, 1, D], F32))
                psF = fst.enter_context(nc.psum_tensor("f_ps", [128, 4096], F32))
                R_wF = Res("wF")
                R_qsrc, R_qa, R_qacc, R_qb = (Res(x) for x in
                                              ("qsrc", "qa", "qacc", "qb"))
                R_qbf = [Res("qbf0"), Res("qbf1")]
                R_qbfX = [Res("qbfX0"), Res("qbfX1")]
                R_qscl = Res("qscl")
                R_xqfm = [Res("xqfm0"), Res("xqfm1")]
                R_hqfm = [Res("hqfm0"), Res("hqfm1")]
                R_silF = Res("silF")
                R_r1F = [Res("r1F0"), Res("r1F1")]
                R_f1F = [Res("f1F0")]
                R_ps_g = Res("ps_gF")
                R_ps_u = Res("ps_uF")
                R_ps_f1 = Res("ps_f1F")
                R_wscr = Res("wscr")

                def emit_quant(src_ap, dst_ap, nblk, scl_off, R_dst):
                    W = nblk * 16
                    s3 = lambda ap: ap.rearrange("p (n b) -> p n b", b=16)
                    scl = qscl[:, scl_off:scl_off + nblk]
                    sclb = scl.unsqueeze(2).broadcast_to([128, nblk, 16])
                    p.step("vector", lambda e: V.tensor_reduce(
                        scl, s3(src_ap), AX.X, ALU.max, apply_absolute_value=True),
                        s_v, reads=[R_qsrc], writes=[R_qscl])
                    p.step("vector", lambda e: V.tensor_scalar(
                        scl, scl, 1.0 / 6.0, 1e-12, ALU.mult, ALU.max),
                        s_v, writes=[R_qscl])
                    p.step("vector", lambda e: V.reciprocal(
                        qa[:, 0:nblk], scl), s_v, reads=[R_qscl], writes=[R_qa])
                    p.step("vector", lambda e: V.scalar_tensor_tensor(
                        qb[:, 0:W], src_ap, -1.0, src_ap, ALU.mult, ALU.max),
                        s_v, reads=[R_qsrc], writes=[R_qb])
                    p.step("vector", lambda e: V.tensor_tensor(
                        s3(qa[:, 0:W] if False else qb[:, 0:W]), s3(qb[:, 0:W]),
                        qa[:, 0:nblk].unsqueeze(2).broadcast_to([128, nblk, 16]),
                        ALU.mult),
                        s_v, reads=[R_qa], writes=[R_qb])
                    p.step("vector", lambda e: V.tensor_scalar(
                        qacc[:, 0:W], qb[:, 0:W], 2.0, 6291456.0,
                        ALU.min, ALU.add),
                        s_v, reads=[R_qb], writes=[R_qacc])
                    p.step("vector", lambda e: V.tensor_scalar(
                        qa[:, 0:W], qb[:, 0:W], 2.5, None, ALU.is_gt),
                        s_v, reads=[R_qb], writes=[R_qa])
                    p.step("vector", lambda e: V.scalar_tensor_tensor(
                        qa[:, 0:W], qb[:, 0:W], 3.5, qa[:, 0:W],
                        ALU.is_gt, ALU.add),
                        s_v, reads=[R_qb], writes=[R_qa])
                    p.step("vector", lambda e: V.scalar_tensor_tensor(
                        qacc[:, 0:W], qacc[:, 0:W], -6291456.0, qa[:, 0:W],
                        ALU.add, ALU.add),
                        s_v, reads=[R_qa], writes=[R_qacc])
                    p.step("vector", lambda e: V.tensor_scalar(
                        qa[:, 0:W], qb[:, 0:W], 5.0, 2.0, ALU.is_gt, ALU.mult),
                        s_v, reads=[R_qb], writes=[R_qa])
                    p.step("vector", lambda e: V.tensor_tensor(
                        qacc[:, 0:W], qacc[:, 0:W], qa[:, 0:W], ALU.add),
                        s_v, reads=[R_qa], writes=[R_qacc])
                    p.step("vector", lambda e: V.tensor_tensor(
                        s3(qacc[:, 0:W]), s3(qacc[:, 0:W]), sclb, ALU.mult),
                        s_v, reads=[R_qscl], writes=[R_qacc])
                    p.step("vector", lambda e: V.tensor_scalar(
                        qa[:, 0:W], src_ap, 0.0, -2.0, ALU.is_lt, ALU.mult),
                        s_v, reads=[R_qsrc], writes=[R_qa])
                    p.step("vector", lambda e: V.scalar_tensor_tensor(
                        dst_ap, qa[:, 0:W], 1.0, qacc[:, 0:W],
                        ALU.add, ALU.mult),
                        s_v, reads=[R_qa, R_qacc], writes=[R_dst])

                for (wsrc, wdst, K_, nch) in ((wfg_h, wqg_h, 1024, 22),
                                              (wfu_h, wqu_h, 1024, 22),
                                              (wfd_h, wqd_h, 2816, 8)):
                    nh = 1 if K_ <= 1408 else 2
                    Kh = K_ // nh
                    for ch in range(nch):
                        for hf in range(nh):
                            qsl = (ch * nh + hf) % 2
                            p.step("sync", lambda e, wsrc=wsrc, ch=ch, Kh=Kh,
                                   hf=hf: e.dma_start(
                                out=qsrc[:, 0:Kh],
                                in_=wsrc[ch * 128:(ch + 1) * 128,
                                         hf * Kh:(hf + 1) * Kh]),
                                s_ld, writes=[R_qsrc], dma=True)
                            emit_quant(qsrc[:, 0:Kh], qbf[:, qsl, 0:Kh],
                                       Kh // 16, 0, R_qbf[qsl])
                            p.step("sync", lambda e, wdst=wdst, ch=ch, Kh=Kh,
                                   hf=hf, qsl=qsl: e.dma_start(
                                out=wdst[ch * 128:(ch + 1) * 128,
                                         hf * Kh:(hf + 1) * Kh],
                                in_=qbf[:, qsl, 0:Kh]),
                                s_st, reads=[R_qbf[qsl]], writes=[R_wscr], dma=True)
                p.step_dmat(_dmq("F4"),
                       [lambda e, j=j: e.dma_start_transpose(
                           wgq[:, j, :], wqg_h[:, j * 128:(j + 1) * 128])
                        for j in range(8)],
                       s_dt, reads=[R_wscr], writes=[R_wF], dma=True)
                p.step_dmat(_dmq("F4"),
                       [lambda e, j=j: e.dma_start_transpose(
                           wuq[:, j, :], wqu_h[:, j * 128:(j + 1) * 128])
                        for j in range(8)],
                       s_dt, reads=[R_wscr], writes=[R_wF], dma=True)
                p.step_dmat(_dmq("F4"),
                       [lambda e, j=j: e.dma_start_transpose(
                           wdq[:, j, :], wqd_h[:, j * 128:(j + 1) * 128])
                        for j in range(22)],
                       s_dt, reads=[R_wscr], writes=[R_wF], dma=True)

                def f1_out(i):
                    p.step("vector", lambda e: V.tensor_copy(
                        f1F[:, 0, :], psF[:, 3072:4096]),
                        s_v, reads=[R_ps_f1], writes=[R_f1F[0]])
                    p.step("gpsimd", lambda e, i=i: e.dma_start(
                        out=out_h[i * 128:(i + 1) * 128, :], in_=f1F[:, 0, :],
                        accum_op=ALU.add),
                        s_st, reads=[R_f1F[0]], writes=[R_outh[i]], dma=True)

                def gu_mm(half, sl):
                    fns = []
                    for k in range(8):
                        fns.append(lambda e, k=k, sl=sl: TE.ldweights(
                            xqfm[:, sl, k, :]))
                        for (cw, c0) in ((512, 0), (512, 512), (384, 1024)):
                            fns.append(lambda e, k=k, half=half, cw=cw, c0=c0,
                                       sl=sl: TE.matmul(
                                psF[:, c0:c0 + cw], xqfm[:, sl, k, :],
                                wgq[:, k, half * 1408 + c0:half * 1408 + c0 + cw],
                                start=(k == 0), stop=(k == 7)))
                            fns.append(lambda e, k=k, half=half, cw=cw, c0=c0,
                                       sl=sl: TE.matmul(
                                psF[:, 1536 + c0:1536 + c0 + cw], xqfm[:, sl, k, :],
                                wuq[:, k, half * 1408 + c0:half * 1408 + c0 + cw],
                                start=(k == 0), stop=(k == 7)))
                    p.step("tensor", fns, s_t,
                           reads=[R_xqfm[sl], R_wF], writes=[R_ps_g, R_ps_u])

                def down_mm(half):
                    fns = []
                    for ic in range(11):
                        fns.append(lambda e, ic=ic, half=half: TE.ldweights(
                            hqfm[:, half, ic, :]))
                        for q2 in range(2):
                            fns.append(lambda e, ic=ic, q2=q2, half=half: TE.matmul(
                                psF[:, 3072 + q2 * 512:3072 + (q2 + 1) * 512],
                                hqfm[:, half, ic, :],
                                wdq[:, half * 11 + ic, q2 * 512:(q2 + 1) * 512],
                                start=(half == 0 and ic == 0),
                                stop=(half == 1 and ic == 10)))
                    p.step("tensor", fns, s_t,
                           reads=[R_hqfm[half], R_wF], writes=[R_ps_f1])

                for i in range(NT):
                    sl = i % 2
                    p._wait("scalar", s_t, s_t.n)
                    p.step("sync", lambda e, i=i, sl=sl: e.dma_start(
                        out=r1F[:, sl, :], in_=r1_h[i * 128:(i + 1) * 128, :]),
                        s_ld, reads=[R_r1h[i]], writes=[R_r1F[sl]], dma=True)
                    p.step("vector", lambda e, i=i, sl=sl: V.tensor_scalar(
                        qsrc[:, 0:1024], r1F[:, sl, :], stat[:, M1 + i:M1 + i + 1],
                        stat[:, RSTD1 + i:RSTD1 + i + 1], ALU.subtract, ALU.mult),
                        s_v, reads=[R_r1F[sl], R_stat], writes=[R_qsrc])
                    emit_quant(qsrc[:, 0:1024], qbfX[:, sl, :], 64, 0, R_qbfX[sl])
                    p.step_dmat(_dmq("F4"),
                           [lambda e, k=k, sl=sl: e.dma_start_transpose(
                               xqfm[:, sl, k, :],
                               qbfX[:, sl, k * 128:(k + 1) * 128])
                            for k in range(8)],
                           s_dt, reads=[R_qbfX[sl]], writes=[R_xqfm[sl]], dma=True)
                    if i > 0:
                        f1_out(i - 1)
                    gu_mm(0, sl)
                    p._wait("scalar", s_t, s_t.n)
                    p.step("scalar", lambda e: S.activation(
                        silF[:], psF[:, 0:1408], AF.Silu),
                        s_sc, reads=[R_ps_g], writes=[R_silF])
                    p.step("vector", lambda e: V.tensor_tensor(
                        qsrc[:, 0:1408], psF[:, 1536:2944], silF[:], ALU.mult),
                        s_v, reads=[R_ps_u, R_silF], writes=[R_qsrc])
                    gu_mm(1, sl)
                    p._wait("scalar", s_t, s_t.n)
                    p.step("scalar", lambda e: S.activation(
                        silF[:], psF[:, 0:1408], AF.Silu),
                        s_sc, reads=[R_ps_g], writes=[R_silF])
                    emit_quant(qsrc[:, 0:1408], qbf[:, 0, 0:1408], 88, 176,
                               R_qbf[0])
                    p.step_dmat(_dmq("F4"),
                           [lambda e, jj=jj: e.dma_start_transpose(
                               hqfm[:, 0, jj, :],
                               qbf[:, 0, jj * 128:(jj + 1) * 128])
                            for jj in range(11)],
                           s_dt, reads=[R_qbf[0]], writes=[R_hqfm[0]], dma=True)
                    p.step("vector", lambda e: V.tensor_tensor(
                        qsrc[:, 0:1408], psF[:, 1536:2944], silF[:], ALU.mult),
                        s_v, reads=[R_ps_u, R_silF], writes=[R_qsrc])
                    emit_quant(qsrc[:, 0:1408], qbf[:, 1, 0:1408], 88, 176,
                               R_qbf[1])
                    p.step_dmat(_dmq("F4"),
                           [lambda e, jj=jj: e.dma_start_transpose(
                               hqfm[:, 1, jj, :],
                               qbf[:, 1, jj * 128:(jj + 1) * 128])
                            for jj in range(11)],
                           s_dt, reads=[R_qbf[1]], writes=[R_hqfm[1]], dma=True)
                    down_mm(0)
                    down_mm(1)
                f1_out(NT - 1)
            p.barrier(ALL)

        if dbg and stages >= 4:
            d_mix = dout("d_mix", (T, D))
            R_dmix = Res("dmix")
            for i in range(NT):
                p.step("sync", lambda e, i=i: e.dma_start(
                    out=d_mix[i * 128:(i + 1) * 128, :],
                    in_=mix_h[i * 128:(i + 1) * 128, :]),
                    s_st, reads=[R_mixh[i]], writes=[R_dmix], dma=True)
        R_out = Res("out")
        if stages < 6:
            for i in range(NT):
                p.step("gpsimd", lambda e, i=i: e.dma_start(
                    out=out_h[i * 128:(i + 1) * 128, :], in_=r1_h[i * 128:(i + 1) * 128, :]),
                    s_st, writes=[R_out], dma=True)
        if dbg:
            d_rtw = dout("d_rtw", (128, NT * 8))
            p.step("gpsimd", lambda e: e.dma_start(out=d_rtw[:, :], in_=rtw[:]),
                   s_st, reads=[R_rtw], writes=[Res("drtw")], dma=True)
        p.barrier([s_st])

        with nc.Block() as block:
            @block.sync
            def _(eng):
                for f in p.q["sync"]:
                    f(eng)

            @block.scalar
            def _(eng):
                for f in p.q["scalar"]:
                    f(eng)

            @block.vector
            def _(eng):
                for f in p.q["vector"]:
                    f(eng)

            @block.tensor
            def _(eng):
                for f in p.q["tensor"]:
                    f(eng)

            @block.gpsimd
            def _(eng):
                for f in p.q["gpsimd"]:
                    f(eng)

    return nc, sorted(dbg_handles)


def host_layouts(inputs, core):
    g = lambda n: np.asarray(inputs[n])
    m = {}
    m["x"] = np.ascontiguousarray(g("L0_attn_x")[core])
    m["L0_Wq"] = g("L0_attn_Wq")
    m["L0_Wk"] = g("L0_attn_Wk")
    wv = g("L0_attn_Wv")
    wv_ev = np.zeros((D, 8, 128), np.float32)
    wv_od = np.zeros((D, 8, 128), np.float32)
    for gq in range(8):
        wv_ev[:, gq, 0:64] = wv[:, (2 * gq) * 64:(2 * gq + 1) * 64]
        wv_od[:, gq, 64:128] = wv[:, (2 * gq + 1) * 64:(2 * gq + 2) * 64]
    m["L0_Wv_ev"] = wv_ev.reshape(D, 1024)
    m["L0_Wv_od"] = wv_od.reshape(D, 1024)
    m["L0_Wo"] = g("L0_attn_Wo")
    bv = g("L0_attn_bv")
    bv_ev = np.zeros((1, 1024), np.float32)
    bv_od = np.zeros((1, 1024), np.float32)
    for gq in range(8):
        bv_ev[0, gq * 128:gq * 128 + 64] = bv[(2 * gq) * 64:(2 * gq + 1) * 64]
        bv_od[0, gq * 128 + 64:gq * 128 + 128] = bv[(2 * gq + 1) * 64:(2 * gq + 2) * 64]
    m["L0_bv_ev"] = bv_ev
    m["L0_bv_od"] = bv_od
    m["L0_bq_fm"] = np.ascontiguousarray(g("L0_attn_bq").reshape(8, 128).T)
    m["L0_bk_fm"] = np.ascontiguousarray(g("L0_attn_bk").reshape(8, 128).T)
    m["L0_bo"] = g("L0_attn_bo").reshape(1, D)
    m["L0_rtr_w"] = g("L0_ffn_router_w")
    m["L0_rtr_b"] = np.broadcast_to(g("L0_ffn_router_bias").astype(np.float32).reshape(1, 1), (128, 1)).copy()
    m["L0_gate_w"] = g("L0_ffn_gate_w")
    m["L0_up_w"] = g("L0_ffn_up_w")
    m["L0_down_w"] = g("L0_ffn_down_w")
    m["L0_sg_w"] = g("L0_ffn_s_gate_w")
    m["L0_su_w"] = g("L0_ffn_s_up_w")
    m["L0_sd_w"] = g("L0_ffn_s_down_w")
    m["L1_Wqkv_a"] = g("L1_attn_Wqkv_a")
    bqkv = g("L1_attn_bqkv_a")
    m["L1_bq_a_fm"] = np.ascontiguousarray(bqkv[0:1024].reshape(8, 128).T)
    m["L1_bk_a_fm"] = np.ascontiguousarray(bqkv[1024:2048].reshape(8, 128).T)
    m["L1_bv_a_row"] = bqkv[2048:3072].reshape(1, D)
    m["L1_Wp_a"] = g("L1_attn_Wp_a")
    m["L1_bp_a_row"] = g("L1_attn_bp_a").reshape(1, D)
    m["L1_g1_fm"] = np.ascontiguousarray(g("L1_attn_g1").reshape(8, 128).T)
    m["L1_b1_fm"] = np.ascontiguousarray(g("L1_attn_b1").reshape(8, 128).T)
    m["L1_Wqkv_s"] = g("L1_attn_Wqkv_s")
    bqkvs = g("L1_attn_bqkv_s")
    m["L1_bq_s_fm"] = np.ascontiguousarray(bqkvs[0:1024].reshape(8, 128).T)
    m["L1_bk_s_fm"] = np.ascontiguousarray(bqkvs[1024:2048].reshape(8, 128).T)
    m["L1_bv_s_row"] = bqkvs[2048:3072].reshape(1, D)
    m["L1_Wo_s"] = g("L1_attn_Wo_s")
    m["L1_bo_s_row"] = g("L1_attn_bo_s").reshape(1, D)
    m["L1_Wo"] = g("L1_attn_Wo")
    m["L1_bo_row"] = g("L1_attn_bo").reshape(1, D)
    m["L1_gm_fm"] = np.ascontiguousarray(g("L1_attn_gm").reshape(8, 128).T)
    m["L1_bm_fm"] = np.ascontiguousarray(g("L1_attn_bm").reshape(8, 128).T)
    m["L1_W1"] = g("L1_attn_W1")
    m["L1_b1m_row"] = g("L1_attn_b1m").reshape(1, 4 * D)
    m["L1_W2"] = g("L1_attn_W2")
    m["L1_b2m_row"] = g("L1_attn_b2m").reshape(1, D)
    m["L1f_w_gate"] = g("L1_ffn_w_gate")
    m["L1f_w_up"] = g("L1_ffn_w_up")
    m["L1f_w_down"] = g("L1_ffn_w_down")
    sA = np.arange(128)[:, None]
    rA = np.arange(128)[None, :]
    m["tril_incl"] = (rA >= sA).astype(BF)
    qr = np.arange(128)[:, None]
    kr = np.arange(128)[None, :]
    swa_mask = np.zeros((128, 256), BF)
    swa_mask[:, 0:128] = (kr > qr).astype(BF)
    swa_mask[:, 128:256] = (kr <= qr).astype(BF)
    swa_mask0 = np.zeros((128, 256), BF)
    swa_mask0[:, 128:256] = (kr <= qr).astype(BF)
    m["swa_mask"] = swa_mask
    m["swa_mask0"] = swa_mask0
    m["ones_row"] = np.ones((1, 128), BF)
    m["eye128"] = np.eye(128, dtype=BF)
    strip = np.zeros((128, 126), BF)
    for qr in range(128):
        strip[qr, 0:(62 + (1 if qr >= 64 else 0)) + 1] = 1
    m["l0_strip"] = strip
    return m


_CACHE = {}


def kernel(**inputs):
    if "nc" not in _CACHE:
        _CACHE["nc"] = build_program()
    nc, dbg_names = _CACHE["nc"]
    in_maps = [host_layouts(inputs, c) for c in range(8)]
    res = run_bass_kernel_spmd(nc, in_maps, core_ids=list(range(8)))
    out = np.stack([np.asarray(r["out"]) for r in res.results], axis=0)
    kernel._dbg = [{k: np.asarray(r[k]) for k in dbg_names} for r in res.results]
    return out
